# version 06
# baseline (speedup 1.0000x reference)
import numpy as np
import concourse.bass as bass
import concourse.mybir as mybir
from concourse.bass_utils import run_bass_kernel_spmd

F32 = mybir.dt.float32
AF = mybir.ActivationFunctionType
ALU = mybir.AluOpType
AX = mybir.AxisListType

C32 = float(3 * 2**22)
C64 = float(3 * 2**51)
EPS0 = 1e-6
LNEPS = 1e-5
NEG = -3.0e38

ENGINES = ("sync", "tensor", "vector", "scalar", "gpsimd")


class S:
    def __init__(self, nc, name):
        self.h = nc.alloc_semaphore(name)
        self.n = 0
        self.name = name

    def bump(self, k=1):
        self.n += k
        return self.n


class Prog:

    def __init__(self, nc):
        self.nc = nc
        self.phases = []
        self.ph = None
        self._vv = {"vector": S(nc, "vvv"), "scalar": S(nc, "vvs")}
        self._vhist = {"vector": [], "scalar": []}
        self.n_edges = 0

    def phase(self, name):
        self.ph = {"name": name, "ops": {e: [] for e in ENGINES},
                   "lastwait": {e: {} for e in ENGINES}}
        self.phases.append(self.ph)

    @staticmethod
    def _box(a):
        try:
            nm = a.tensor.name
            ap = a.ap
            pitch, np_ = ap[0]
            off = a.offset
            p0, f0 = off // pitch, off % pitch
            f1 = f0 + 1
            for s, n in ap[1:]:
                if s >= pitch:
                    return None
                f1 += (n - 1) * s
            if f1 > pitch:
                return None
            return (nm, p0, p0 + np_, f0, f1)
        except Exception:
            return None

    @staticmethod
    def _ovl(w, r):
        if w is None or r is None:
            return True
        return (w[0] == r[0] and w[1] < r[2] and r[1] < w[2]
                and w[3] < r[4] and r[3] < w[4])

    def op(self, eng, fn, wait=(), inc=None, inc_n=1, reads=None, writes=None):
        rec = {"extra": False}
        hist = self._vhist.get(eng)
        if hist is not None:
            vv = self._vv[eng]
            rb = [None] if reads is None else [self._box(a) for a in reads
                                               if a is not None]
            wb = [None] if writes is None else [self._box(a) for a in writes
                                                if a is not None]
            need = None
            for p in reversed(hist[-16:]):
                if (any(self._ovl(w, r) for w in p["w"] for r in rb)
                        or any(self._ovl(w, w2) for w in p["w"] for w2 in wb)
                        or any(self._ovl(r0, w2) for r0 in p["r"] for w2 in wb)):
                    need = p
                    break
            if need is not None:
                if need["tok"] is None:
                    need["rec"]["extra"] = True
                    need["tok"] = (vv, vv.bump())
                wait = list(wait) + [need["tok"]]
                self.n_edges += 1
        lw = self.ph["lastwait"][eng]
        wl = []
        for w in wait:
            if w is None:
                continue
            (s, v) = w
            if lw.get(s.name, -1) < v:
                wl.append((s.h, v))
                lw[s.name] = v
        tok = None
        if inc is not None:
            tok = (inc, inc.bump(inc_n))
        vvh = self._vv[eng].h if hist is not None else None

        def f(e):
            for sh, v in wl:
                e.wait_ge(sh, v)
            r = fn(e)
            if inc is not None:
                r.then_inc(inc.h, inc_n)
            if rec["extra"]:
                r.then_inc(vvh, 1)
        self.ph["ops"][eng].append(f)
        if hist is not None:
            hist.append({"w": wb, "r": rb, "tok": tok, "rec": rec})
            if len(hist) > 48:
                del hist[:16]
        return tok

    def dma(self, q, out, in_, sem, wait=(), slow=False):
        if slow:
            return self.op(q, lambda e: e.dma_start(out=out, in_=in_,
                                                    allow_slow_non_contiguous=True),
                           wait, sem, 16)
        return self.op(q, lambda e: e.dma_start(out=out, in_=in_), wait, sem, 16)

    def mm(self, out, lhsT, rhs, start=True, stop=True, wait=(), inc=None):
        return self.op("tensor",
                       lambda e: e.matmul(out, lhsT, rhs, start=start, stop=stop),
                       wait, inc)

    def tp(self, out, in_, ident, np_, wait=(), inc=None):
        return self.op("tensor",
                       lambda e: e.matmul(out, in_, ident[0:np_, 0:np_],
                                          start=True, stop=True, is_transpose=True),
                       wait, inc)

    def act(self, out, in_, func, bias=None, scale=None, accum_out=None, wait=(), inc=None):
        kw = {}
        rd = [in_]
        if bias is not None:
            kw["bias"] = bias
            if hasattr(bias, "offset"):
                rd.append(bias)
        if scale is not None:
            kw["scale"] = scale
            if hasattr(scale, "offset"):
                rd.append(scale)
        wr = [out]
        if accum_out is not None:
            kw["accum_out"] = accum_out
            wr.append(accum_out)
        return self.op("scalar", lambda e: e.activation(out, in_, func, **kw), wait, inc,
                       reads=rd, writes=wr)

    def cp(self, eng, out, in_, wait=(), inc=None):
        return self.op(eng, lambda e: e.tensor_copy(out, in_), wait, inc,
                       reads=[in_], writes=[out])

    def ts(self, out, in_, s1, s2, op0, op1=None, wait=(), inc=None, eng="vector"):
        if op1 is None:
            return self.op(eng, lambda e: e.tensor_scalar(out, in_, s1, None, op0),
                           wait, inc, reads=[in_], writes=[out])
        return self.op(eng, lambda e: e.tensor_scalar(out, in_, s1, s2, op0, op1),
                       wait, inc, reads=[in_], writes=[out])

    def tt(self, out, a, b, alu, wait=(), inc=None, eng="vector"):
        return self.op(eng, lambda e: e.tensor_tensor(out, a, b, alu), wait, inc,
                       reads=[a, b], writes=[out])

    def stt(self, out, in0, scal, in1, op0, op1, wait=(), inc=None, eng="vector"):
        return self.op(eng, lambda e: e.scalar_tensor_tensor(out, in0, scal, in1, op0, op1),
                       wait, inc, reads=[in0, scal, in1], writes=[out])

    def red(self, out, in_, alu, absval=None, negate=None, wait=(), inc=None):
        return self.op("vector",
                       lambda e: e.tensor_reduce(out, in_, AX.X, alu,
                                                 apply_absolute_value=absval, negate=negate),
                       wait, inc, reads=[in_], writes=[out])

    def recip(self, out, in_, wait=(), inc=None):
        return self.op("vector", lambda e: e.reciprocal(out, in_), wait, inc,
                       reads=[in_], writes=[out])

    def memset(self, ap, val, wait=(), inc=None, eng="gpsimd"):
        return self.op(eng, lambda e: e.memset(ap, val), wait, inc,
                       reads=[], writes=[ap])

    def coll(self, kind, alu, groups, ins, outs, wait=(), inc=None):
        return self.op("gpsimd",
                       lambda e: e.collective_compute(kind, alu, replica_groups=groups,
                                                      ins=ins, outs=outs),
                       wait, inc)

    def emit(self):
        nc = self.nc
        for ph in self.phases:
            with nc.Block(ph["name"]) as block:
                for ename in ENGINES:
                    ops = ph["ops"][ename]
                    if not ops:
                        continue

                    def body(eng, ops=ops):
                        for f in ops:
                            f(eng)
                    getattr(block, ename)(body)


class Bank:

    def __init__(self, pb):
        self.pb = pb
        self.readers = []
        self.wtok = None

    def wwait(self):
        w, self.readers = self.readers, []
        return w

    def wrote(self, tok):
        self.wtok = tok

    def rwait(self):
        return [self.wtok] if self.wtok else []

    def read(self, tok):
        self.readers.append(tok)


def build_nc(T=4096, CTA=256, blocks=("l0a", "l0f", "l1a", "l1f"),
             detect_races=False, n_cores=8, dbg=()):
    nc = bass.Bass("TRN2", target_bir_lowering=False,
                   detect_race_conditions=detect_races)
    D, H, HD = 1024, 16, 64
    NCH_A = T // CTA
    NTIL_A = CTA // 128

    x_h = nc.dram_tensor("x", [D, T], F32, kind="ExternalInput").ap()
    y_h = nc.dram_tensor("y", [D, T], F32, kind="ExternalOutput").ap()
    wqkv0 = nc.dram_tensor("l0_wqkv", [D, 3 * D], F32, kind="ExternalInput").ap()
    wout0 = nc.dram_tensor("l0_wout", [D, D], F32, kind="ExternalInput").ap()
    proj0 = nc.dram_tensor("l0_proj", [H, 128, HD], F32, kind="ExternalInput").ap()
    ident_h = nc.dram_tensor("ident", [128, 128], F32, kind="ExternalInput").ap()
    hones_h = nc.dram_tensor("hones", [128, 8 * H], F32, kind="ExternalInput").ap()
    indb_h = nc.dram_tensor("indb", [16, 8 * 128], F32, kind="ExternalInput").ap()
    indh_h = nc.dram_tensor("indh", [16, 16 * 64], F32, kind="ExternalInput").ap()
    w1 = nc.dram_tensor("l0_w1", [4096, D], F32, kind="ExternalInput").ap()
    b1 = nc.dram_tensor("l0_b1", [4096], F32, kind="ExternalInput").ap()
    w2 = nc.dram_tensor("l0_w2", [D, 4096], F32, kind="ExternalInput").ap()
    b2 = nc.dram_tensor("l0_b2", [D], F32, kind="ExternalInput").ap()
    qkvw1 = nc.dram_tensor("l1_qkvw", [3 * D, D], F32, kind="ExternalInput").ap()
    qkvb1 = nc.dram_tensor("l1_qkvb", [3 * D], F32, kind="ExternalInput").ap()
    projw1 = nc.dram_tensor("l1_projw", [D, D], F32, kind="ExternalInput").ap()
    projb1 = nc.dram_tensor("l1_projb", [D], F32, kind="ExternalInput").ap()
    pm1 = nc.dram_tensor("l1_pm", [64, HD], F32, kind="ExternalInput").ap()
    wg1 = nc.dram_tensor("l1_wg", [3072, D], F32, kind="ExternalInput").ap()
    wu1 = nc.dram_tensor("l1_wu", [3072, D], F32, kind="ExternalInput").ap()
    wd1 = nc.dram_tensor("l1_wd", [D, 3072], F32, kind="ExternalInput").ap()

    kv0_bi = nc.dram_tensor("kv0_bi", [128, 2080], F32)
    kv0_bo = nc.dram_tensor("kv0_bo", [128, 2080], F32)
    mh0_bi = nc.dram_tensor("mh0_bi", [16, 1], F32)
    mh0_bo = nc.dram_tensor("mh0_bo", [16, 1], F32)
    wq1t_sc = nc.dram_tensor("wq1t_sc", [128, 32 * 1024], F32)
    wq2t_sc = nc.dram_tensor("wq2t_sc", [128, 32 * 1024], F32)
    qkv1t_sc = nc.dram_tensor("qkv1t_sc", [128, 8 * 3072], F32)
    proj1t_sc = nc.dram_tensor("proj1t_sc", [128, 8 * 1024], F32)
    kv1_bi = nc.dram_tensor("kv1_bi", [64, 1040], F32)
    kv1_bo = nc.dram_tensor("kv1_bo", [64, 1040], F32)
    row_bi = nc.dram_tensor("row_bi", [1, 2048], F32)
    row_bo = nc.dram_tensor("row_bo", [1, 2048], F32)
    wut_sc = nc.dram_tensor("wut_sc", [128, 24 * 1024], F32)
    wgt_sc = nc.dram_tensor("wgt_sc", [128, 24 * 1024], F32)
    wdt_sc = nc.dram_tensor("wdt_sc", [128, 24 * 1024], F32)
    upre_sc = nc.dram_tensor("upre_sc", [128, 24 * T], F32)
    gpre_sc = nc.dram_tensor("gpre_sc", [128, 24 * T], F32)
    m2_sc = nc.dram_tensor("m2_sc", [128, 24 * T], F32)
    mx_bi = nc.dram_tensor("mx_bi", [1, 4], F32)
    mx_bo = nc.dram_tensor("mx_bo", [1, 4], F32)

    sb = nc.alloc_sbuf_tensor
    RES = sb("RES", [128, 8 * T], F32)
    XNKQ = sb("XNKQ", [128, 4096], F32)
    XN = XNKQ[:, 0:8 * CTA]
    KQ = XNKQ[:, 8 * CTA:16 * CTA]
    UA = sb("UA", [128, 2048], F32)
    EQF = sb("EQF", [128, 2 * CTA], F32)
    EKT = EQF[:, 0:2 * CTA]
    KV0 = sb("KV0", [128, 2080], F32)
    STA = sb("STA", [128, 672], F32)
    STB = sb("STB", [16, 2 * CTA], F32)
    WS = sb("WS", [128, 4096], F32)
    SQS = sb("SQS", [128, 2 * CTA], F32)
    LNR = sb("LNR", [1, 2 * CTA], F32)
    CONST = sb("CONST", [128, 576], F32)
    PMT0 = sb("PMT0", [128, 8 * 128], F32)
    INDB = sb("INDB", [16, 8 * 128], F32)
    INDH = sb("INDH", [16, 16 * 64], F32)

    VT = UA[:, 0:NTIL_A * 1024]
    NUMF = UA[:, 0:8 * CTA]

    IDENT = CONST[:, 0:128]
    HONES = CONST[:, 128:256]
    ONES = CONST[:, 256:384]
    GAM = CONST[:, 384:385]
    ONE1 = CONST[:, 385:386]

    B1S = CONST[:, 400:432]
    B2S = CONST[:, 432:440]
    RG1 = CONST[:, 440:441]
    RG2 = CONST[:, 441:442]
    QBS = CONST[:, 448:472]
    PJBS = CONST[:, 472:480]
    PMT1 = CONST[:, 480:544]
    GAB = CONST[:, 544:576]
    MKA = STA[:, 0:16]
    DKT = STA[:, 16:32]
    NDKT = STA[:, 32:48]
    MQT = STA[:, 48:64]
    WNT = STA[:, 64:80]
    MH16 = STA[0:16, 88:89]
    EH16 = STA[0:16, 89:90]
    SCR1 = STA[:, 96:112]
    SCRR = STA[:, 112:113]
    MKROW = STA[0:16, 128:256]

    RDEN = STB[0:16, 0:CTA]
    DQF = STB[0:16, CTA:2 * CTA]
    RMXR = STB[0:16, CTA:2 * CTA]
    KV1 = KV0[0:64, 0:1040]
    KSR = KV0[64:65, 0:1024]
    QBV = KV0[64:65, 1040:2064]
    VSFR = KV0[65:66, 1040:2064]
    ONE65 = KV0[65:66, 0:1024]
    VSUM = KV0[96:97, 0:1024]
    BSTG = STA[0:1, 384:384 + CTA]
    ESB2 = STA[:, 640:656]
    XSUM = STA[:, 344:352]
    ESB = STA[:, 256:272]
    ETB = STA[:, 272:288]
    EWTM = STA[:, 256:288]
    BTT = STA[:, 288:320]
    DENT = STA[:, 320:336]
    EHB = STA[:, 336:352]
    EHROW = STA[0:1, 352:368]

    pt = nc.alloc_psum_tensor
    PB = [pt(f"PB{i}", [128, 512], F32) for i in range(8)]
    BK = [Bank(PB[i]) for i in range(8)]

    P = Prog(nc)
    ld = S(nc, "ld")
    ld2 = S(nc, "ld2")
    st = S(nc, "st")
    te = S(nc, "te")
    ve = S(nc, "ve")
    ac = S(nc, "ac")
    gp = S(nc, "gp")
    cc = S(nc, "cc")

    P.phase("init")
    x_toks = []
    for c in range(8):
        x_toks.append(P.dma("sync", RES[:, c * T:(c + 1) * T],
                            x_h[c * 128:(c + 1) * 128, :], ld))
    t_id = P.dma("sync", IDENT, ident_h[:, :], ld)
    P.dma("sync", HONES, hones_h[:, :], ld)
    t_hon = (ld, ld.n)
    P.dma("sync", INDB[0:16, :], indb_h[:, :], ld)
    t_indb = (ld, ld.n)
    P.memset(ONES, 1.0, inc=gp)
    P.memset(ONE1, 1.0, inc=gp)
    t_ones = (gp, gp.n)
    P.ts(GAM, ONE1, C64, C64, ALU.add, ALU.subtract, wait=[t_ones])
    t_gam = P.tt(GAM, ONE1, GAM, ALU.subtract, inc=ve)
    pm_stage = WS[:, 0:64]
    pm_tmp = WS[:, 64:192]
    t_pmt = None
    for h in range(H):
        t_pm = P.dma("sync", pm_stage, proj0[h, :, :], ld,
                     wait=[t_pmt] if t_pmt else [])
        t_tp = P.tp(PB[0][0:64, 0:128], pm_stage[0:128, 0:64], IDENT, 128,
                    wait=[t_pm, t_id] + BK[0].wwait(), inc=te)
        BK[0].wrote(t_tp)
        hp, r0 = h // 2, 64 * (h % 2)
        if r0 == 0:
            t_pmt = P.act(PMT0[0:64, hp * 128:(hp + 1) * 128], PB[0][0:64, 0:128],
                          AF.Identity, scale=0.125, wait=[t_tp], inc=ac)
            BK[0].read(t_pmt)
        else:
            t_ev = P.act(pm_tmp[0:64, 0:128], PB[0][0:64, 0:128],
                         AF.Identity, scale=0.125, wait=[t_tp], inc=ac)
            BK[0].read(t_ev)
            t_pmt = P.dma("gpsimd", PMT0[64:128, hp * 128:(hp + 1) * 128],
                          pm_tmp[0:64, 0:128], st, wait=[t_ev])

    def ln_chunk(cols, xn_out, bs, bb, resid_ready):
        ct = cols.stop - cols.start
        t_sum = None
        for dc in range(8):
            rchunk = RES[:, dc * T + cols.start: dc * T + cols.stop]
            t_sq = P.act(SQS[:, (dc % 2) * ct:(dc % 2 + 1) * ct], rchunk, AF.Square,
                         wait=[resid_ready, (te, te.n)], inc=ac)
            w0 = (BK[bs].wwait() if dc == 0 else []) + [t_sq]
            P.mm(BK[bs].pb[0:1, 0:ct], ONES[:, 0:1], rchunk,
                 start=(dc == 0), stop=False, wait=w0, inc=te)
            t_sum = P.mm(BK[bs].pb[0:1, ct:2 * ct], ONES[:, 0:1],
                         SQS[:, (dc % 2) * ct:(dc % 2 + 1) * ct],
                         start=False, stop=(dc == 7), inc=te)
        BK[bs].wrote(t_sum)
        P.ts(LNR[0:1, 0:ct], BK[bs].pb[0:1, 0:ct], 1.0 / 1024.0, None, ALU.mult,
             wait=[t_sum])
        t_st2 = P.ts(LNR[0:1, ct:2 * ct], BK[bs].pb[0:1, ct:2 * ct], 1.0 / 1024.0, None,
                     ALU.mult, inc=ve)
        BK[bs].read(t_st2)
        P.tt(SQS[0:1, 0:ct], LNR[0:1, 0:ct], LNR[0:1, 0:ct], ALU.mult)
        P.tt(SQS[0:1, 0:ct], LNR[0:1, ct:2 * ct], SQS[0:1, 0:ct], ALU.subtract)
        t_var = P.ts(SQS[0:1, 0:ct], SQS[0:1, 0:ct], LNEPS, None, ALU.add, inc=ve)
        t_sqr = P.act(SQS[0:1, ct:2 * ct], SQS[0:1, 0:ct], AF.Sqrt, wait=[t_var], inc=ac)
        P.recip(LNR[0:1, ct:2 * ct], SQS[0:1, ct:2 * ct], wait=[t_sqr])
        P.tt(LNR[0:1, 0:ct], LNR[0:1, 0:ct], LNR[0:1, ct:2 * ct], ALU.mult)
        t_ab = P.ts(LNR[0:1, 0:ct], LNR[0:1, 0:ct], -1.0, None, ALU.mult, inc=ve)
        P.mm(BK[bb].pb[0:128, 0:ct], ONES[0:1, 0:128], LNR[0:1, ct:2 * ct],
             wait=[t_ab] + BK[bb].wwait())
        t_bc = P.mm(BK[bb].pb[0:128, ct:2 * ct], ONES[0:1, 0:128], LNR[0:1, 0:ct], inc=te)
        BK[bb].wrote(t_bc)
        t_xn = None
        for dc in range(8):
            rchunk = RES[:, dc * T + cols.start: dc * T + cols.stop]
            P.tt(xn_out[:, dc * ct:(dc + 1) * ct], rchunk, BK[bb].pb[0:128, 0:ct],
                 ALU.mult, wait=[t_bc])
            t_xn = P.tt(xn_out[:, dc * ct:(dc + 1) * ct], xn_out[:, dc * ct:(dc + 1) * ct],
                        BK[bb].pb[0:128, ct:2 * ct], ALU.add, inc=ve)
        BK[bb].read(t_xn)
        return t_xn

    def dstat(src, cols_ct, bs, src_ready):
        t_d = None
        for dc in range(8):
            t_sq = P.act(SQS[:, (dc % 2) * cols_ct:(dc % 2 + 1) * cols_ct],
                         src[:, dc * cols_ct:(dc + 1) * cols_ct], AF.Square,
                         wait=[src_ready, (te, te.n)], inc=ac)
            w0 = (BK[bs].wwait() if dc == 0 else []) + [t_sq, t_hon]
            t_d = P.mm(BK[bs].pb[0:16, 0:cols_ct], HONES[:, dc * 16:(dc + 1) * 16],
                       SQS[:, (dc % 2) * cols_ct:(dc % 2 + 1) * cols_ct],
                       start=(dc == 0), stop=(dc == 7), wait=w0, inc=te)
        BK[bs].wrote(t_d)
        t_dc = P.cp("vector", DQF, BK[bs].pb[0:16, 0:cols_ct], wait=[t_d], inc=ve)
        BK[bs].read(t_dc)
        return t_dc

    resid_ready = x_toks[-1]

    if "l0a" in blocks:
        P.phase("l0a")
        P.memset(MKA, NEG, inc=gp)
        P.memset(bass.AP(tensor=KV0, offset=64, ap=[[2080, 128], [65, 32]]), 0.0, inc=gp)
        t_init = (gp, gp.n)
        wk_slot = [WS[:, 0:1024], WS[:, 1024:2048]]
        wv_slot = WS[:, 2048:4096]
        ks_last = [None, None]
        ekt_free = [None, None]
        kv_open = [False] * 4
        t_kv_last = [None] * 4
        for ch in range(NCH_A):
            cols = slice(ch * CTA, (ch + 1) * CTA)
            t_xn = ln_chunk(cols, XN, 5, 5, resid_ready)
            for kc in range(8):
                sl = kc % 2
                w_ap = bass.AP(tensor=wqkv0.tensor, offset=(D + kc * 128),
                               ap=[[3 * D, 128], [128 * 3 * D, 8], [1, 128]])
                t_wk = P.dma("sync",
                             wk_slot[sl].rearrange("p (c j) -> p c j", j=128),
                             w_ap, ld, wait=[ks_last[sl]] if ks_last[sl] else [])
                bk = BK[0]
                for dc in range(8):
                    w0 = ([t_wk, t_xn] + bk.wwait()) if dc == 0 else []
                    t_kmm = P.mm(bk.pb[0:128, 0:CTA],
                                 wk_slot[sl][:, dc * 128:(dc + 1) * 128],
                                 XN[:, dc * CTA:(dc + 1) * CTA],
                                 start=(dc == 0), stop=(dc == 7), wait=w0,
                                 inc=te if dc == 7 else None)
                bk.wrote(t_kmm)
                t_kev = P.cp("vector", KQ[:, kc * CTA:(kc + 1) * CTA],
                             bk.pb[0:128, 0:CTA], wait=[t_kmm], inc=ve)
                bk.read(t_kev)
                ks_last[sl] = t_kev
            t_k = (ve, ve.n)
            if "nov" not in dbg:
                for vg in range(4):
                    w_ap = bass.AP(tensor=wqkv0.tensor, offset=(2 * D + vg * 256),
                                   ap=[[3 * D, 128], [128 * 3 * D, 8], [1, 256]])
                    t_wv = P.dma("gpsimd",
                                 wv_slot.rearrange("p (c j) -> p c j", j=256),
                                 w_ap, ld2, wait=[(te, te.n)])
                    for tl in range(NTIL_A):
                        bk = BK[0]
                        for dc in range(8):
                            w0 = ([t_wv, t_xn] + bk.wwait()) if dc == 0 else []
                            t_vmm = P.mm(bk.pb[0:128, 0:256],
                                         XN[:, dc * CTA + tl * 128: dc * CTA + (tl + 1) * 128],
                                         wv_slot[:, dc * 256:(dc + 1) * 256],
                                         start=(dc == 0), stop=(dc == 7), wait=w0,
                                         inc=te if dc == 7 else None)
                        bk.wrote(t_vmm)
                        t_vev = P.act(VT[:, tl * 1024 + vg * 256: tl * 1024 + (vg + 1) * 256],
                                      bk.pb[0:128, 0:256], AF.Identity, wait=[t_vmm], inc=ac)
                        bk.read(t_vev)
            t_v = (ac, ac.n)
            if "nofeat" in dbg:
                continue
            t_dkc = dstat(KQ, CTA, 5, t_k)
            for tl in range(NTIL_A):
                t_tpd = P.tp(BK[5].pb[0:128, 0:16], DQF[0:16, tl * 128:(tl + 1) * 128],
                             IDENT, 16, wait=[t_dkc] + BK[5].wwait(), inc=te)
                BK[5].wrote(t_tpd)
                t_dkt = P.cp("vector", DKT, BK[5].pb[0:128, 0:16], wait=[t_tpd], inc=ve)
                BK[5].read(t_dkt)
                t_ndk = P.ts(NDKT, DKT, -1.0, None, ALU.mult, inc=ve)
                for h in range(H if "nou" not in dbg else 0):
                    hp, r0 = h // 2, 64 * (h % 2)
                    ub = 2 + (h % 2)
                    t_u = P.mm(BK[ub].pb[0:128, 0:128],
                               KQ[r0:r0 + 64, hp * CTA + tl * 128: hp * CTA + (tl + 1) * 128],
                               PMT0[r0:r0 + 64, hp * 128:(hp + 1) * 128],
                               wait=[t_ndk, t_v, t_init] + BK[ub].wwait(), inc=te)
                    BK[ub].wrote(t_u)
                    P.red(SCRR, BK[ub].pb[0:128, 0:128], ALU.max, wait=[t_u])
                    t_mk = P.tt(MKA[:, h:h + 1], MKA[:, h:h + 1], SCRR, ALU.max, inc=ve)
                    BK[ub].read(t_mk)
                    es = h % 2
                    if "noek" in dbg:
                        continue
                    P.act(EKT[:, es * 256:es * 256 + 128], BK[ub].pb[0:128, 0:128],
                          AF.Exp, bias=NDKT[:, h:h + 1], scale=1.0,
                          wait=[t_u, t_mk] + ([ekt_free[es]] if ekt_free[es] else []))
                    t_ek = P.act(EKT[:, es * 256 + 128:es * 256 + 256],
                                 BK[ub].pb[0:128, 0:128], AF.Exp,
                                 bias=NDKT[:, h:h + 1], scale=-1.0, inc=ac)
                    BK[ub].read(t_ek)
                    for fc in range(2):
                        if "nokv" in dbg:
                            continue
                        kb = (1, 4, 6, 7)[2 * fc + h // 8]
                        kbi = (1, 4, 6, 7).index(kb)
                        stt0 = not kv_open[kbi]
                        kv_open[kbi] = True
                        is_last = (ch == NCH_A - 1 and tl == NTIL_A - 1
                                   and (h % 8) == 7)
                        t_kv = P.mm(PB[kb][0:128, (h % 8) * 64:(h % 8) * 64 + 64],
                                    EKT[:, es * 256 + fc * 128:es * 256 + (fc + 1) * 128],
                                    VT[:, tl * 1024 + h * 64: tl * 1024 + (h + 1) * 64],
                                    start=stt0, stop=is_last, wait=[t_ek], inc=te)
                        t_kv_last[kbi] = t_kv
                        ekt_free[es] = t_kv
                        t_ks = P.mm(BK[ub].pb[0:128, 256 + fc * 64 + (h // 2) * 2:
                                              257 + fc * 64 + (h // 2) * 2],
                                    EKT[:, es * 256 + fc * 128:es * 256 + (fc + 1) * 128],
                                    ONES[:, 0:1], start=True, stop=True,
                                    wait=[t_ek], inc=te)
                        BK[ub].wrote(t_ks)
                        kc0 = 65 * (fc * 16 + h) + 64
                        t_ka = P.tt(KV0[:, kc0:kc0 + 1], KV0[:, kc0:kc0 + 1],
                                    BK[ub].pb[0:128, 256 + fc * 64 + (h // 2) * 2:
                                              257 + fc * 64 + (h // 2) * 2],
                                    ALU.add, wait=[t_ks], inc=ve)
                        BK[ub].read(t_ka)
        if "notail" not in dbg:
            t_kvev = None
            if "noevac" not in dbg:
                for i, kb in enumerate((1, 4, 6, 7)):
                    out_ap = bass.AP(tensor=KV0, offset=65 * (8 * i),
                                     ap=[[2080, 128], [65, 8], [1, 64]])
                    in_ap = PB[kb].ap().rearrange("p (b j) -> p b j", j=64)
                    t_kvev = P.cp("vector", out_ap, in_ap, wait=[t_kv_last[i]], inc=ve)
                    BK[kb].read(t_kvev)
                    BK[kb].wtok = None
            t_tpm = P.tp(BK[5].pb[0:16, 0:128], MKA, IDENT, 128,
                         wait=[(ve, ve.n)] + BK[5].wwait(), inc=te)
            BK[5].wrote(t_tpm)
            t_mkc = P.cp("vector", MKROW, BK[5].pb[0:16, 0:128], wait=[t_tpm], inc=ve)
            BK[5].read(t_mkc)
            t_mh = P.red(MH16, MKROW, ALU.max, wait=[t_mkc], inc=ve)
            if "nocoll" in dbg:
                t_kvr = (ve, ve.n)
                t_l2 = t_mh
            else:
                t_s1 = P.dma("gpsimd", kv0_bi[:, :], KV0[:, :], st,
                             wait=[t_kvev, (ve, ve.n)])
                t_s2 = P.dma("gpsimd", mh0_bi[:, :], MH16, st, wait=[t_mh])
                pairs_g = [[2 * i, 2 * i + 1] for i in range(n_cores // 2)]
                t_c1 = P.coll("AllReduce", ALU.add, pairs_g, [kv0_bi[:, :].opt()],
                              [kv0_bo[:, :].opt()], wait=[t_s1], inc=cc)
                t_c2 = P.coll("AllReduce", ALU.max, pairs_g, [mh0_bi[:, :].opt()],
                              [mh0_bo[:, :].opt()], wait=[t_s2], inc=cc)
                t_kvr = P.dma("gpsimd", KV0[:, :], kv0_bo[:, :], st, wait=[t_c1])
                t_l2 = P.dma("gpsimd", MH16, mh0_bo[:, :], st, wait=[t_c2])
            if "noeh" not in dbg:
                t_ehx = P.act(EH16, MH16, AF.Exp, wait=[t_l2], inc=ac)
                t_eh0 = P.ts(EH16, EH16, float(256.0 * EPS0), None, ALU.mult, wait=[t_ehx], inc=ve)
                t_tpeh = P.tp(BK[5].pb[0:1, 0:16], EH16, IDENT, 16,
                              wait=[t_eh0] + BK[5].wwait(), inc=te)
                BK[5].wrote(t_tpeh)
                t_ehr = P.cp("vector", EHROW, BK[5].pb[0:1, 0:16], wait=[t_tpeh], inc=ve)
                BK[5].read(t_ehr)
                t_bceh = P.mm(BK[5].pb[0:128, 16:32], ONES[0:1, 0:128], EHROW,
                              wait=[t_ehr] + BK[5].wwait(), inc=te)
                BK[5].wrote(t_bceh)
                t_eh = P.cp("vector", EHB, BK[5].pb[0:128, 16:32], wait=[t_bceh], inc=ve)
                BK[5].read(t_eh)


        if "notail" not in dbg and "nob" not in dbg:
            P.phase("l0b")
            wq_slot = [WS[:, 0:1024], WS[:, 1024:2048]]
            wo_slot = [WS[:, 2048:3072], WS[:, 3072:4096]]
            q_last = [None, None]
            wo_last = [None, None]
            eqf_free = None
            for ch in range(NCH_A):
                cols = slice(ch * CTA, (ch + 1) * CTA)
                t_xn = ln_chunk(cols, XN, 5, 5, resid_ready)
                for qc in range(8):
                    sl = qc % 2
                    w_ap = bass.AP(tensor=wqkv0.tensor, offset=qc * 128,
                                   ap=[[3 * D, 128], [128 * 3 * D, 8], [1, 128]])
                    t_wq = P.dma("sync",
                                 wq_slot[sl].rearrange("p (c j) -> p c j", j=128),
                                 w_ap, ld, wait=[q_last[sl]] if q_last[sl] else [])
                    bk = BK[sl]
                    for dc in range(8):
                        w0 = ([t_wq, t_xn] + bk.wwait()) if dc == 0 else []
                        t_qmm = P.mm(bk.pb[0:128, 0:CTA],
                                     wq_slot[sl][:, dc * 128:(dc + 1) * 128],
                                     XN[:, dc * CTA:(dc + 1) * CTA],
                                     start=(dc == 0), stop=(dc == 7), wait=w0,
                                     inc=te if dc == 7 else None)
                    bk.wrote(t_qmm)
                    t_qev = P.cp("vector", KQ[:, qc * CTA:(qc + 1) * CTA],
                                 bk.pb[0:128, 0:CTA], wait=[t_qmm], inc=ve)
                    bk.read(t_qev)
                    q_last[sl] = t_qev
                t_q = (ve, ve.n)
                t_dqc = dstat(KQ, CTA, 5, t_q)
                for tl in range(NTIL_A):
                    t_tpd = P.tp(BK[5].pb[0:128, 0:16], DQF[0:16, tl * 128:(tl + 1) * 128],
                                 IDENT, 16, wait=[t_dqc] + BK[5].wwait(), inc=te)
                    BK[5].wrote(t_tpd)
                    t_dqt = P.cp("vector", DKT, BK[5].pb[0:128, 0:16], wait=[t_tpd], inc=ve)
                    BK[5].read(t_dqt)
                    for h in range(H):
                        hp, r0 = h // 2, 64 * (h % 2)
                        ub = 2 + (h % 2)
                        t_u = P.mm(BK[ub].pb[0:128, 0:128],
                                   KQ[r0:r0 + 64, hp * CTA + tl * 128: hp * CTA + (tl + 1) * 128],
                                   PMT0[r0:r0 + 64, hp * 128:(hp + 1) * 128],
                                   wait=BK[ub].wwait() + [t_dqt], inc=te)
                        BK[ub].wrote(t_u)
                        t_mq = P.red(MQT[:, h:h + 1], BK[ub].pb[0:128, 0:128], ALU.max,
                                     wait=[t_u], inc=ve)
                        BK[ub].read(t_mq)
                    P.tt(WNT, DKT, MQT, ALU.add)
                    t_wn = P.ts(WNT, WNT, 60.0, None, ALU.min, inc=ve)
                    t_ew = P.act(EWTM[:, 16 * tl:16 * tl + 16], WNT, AF.Exp,
                                 wait=[t_wn], inc=ac)
                t_ewr = (ac, ac.n)
                for h in range(H):
                    hp, r0 = h // 2, 64 * (h % 2)
                    ub = 2 + (h % 2)
                    t_uT = P.mm(BK[ub].pb[0:128, 0:CTA],
                                PMT0[r0:r0 + 64, hp * 128:(hp + 1) * 128],
                                KQ[r0:r0 + 64, hp * CTA: hp * CTA + CTA],
                                wait=BK[ub].wwait(), inc=te)
                    BK[ub].wrote(t_uT)
                    P.act(EQF[:, 0:CTA], BK[ub].pb[0:128, 0:CTA], AF.Exp, scale=1.0,
                          wait=[t_uT] + ([eqf_free] if eqf_free else []))
                    t_eq = P.act(EQF[:, CTA:2 * CTA], BK[ub].pb[0:128, 0:CTA], AF.Exp,
                                 scale=-1.0, inc=ac)
                    BK[ub].read(t_eq)
                    ab = 4 + 3 * (h % 2)
                    w0 = [t_eq, t_kvr] + BK[ab].wwait()
                    P.mm(BK[ab].pb[r0:r0 + 64, 0:CTA], KV0[:, 65 * h:65 * h + 64],
                         EQF[:, 0:CTA], start=True, stop=False, wait=w0)
                    P.mm(BK[ab].pb[r0:r0 + 64, 0:CTA],
                         KV0[:, 65 * (16 + h):65 * (16 + h) + 64],
                         EQF[:, CTA:2 * CTA], start=False, stop=True)
                    P.mm(BK[ab].pb[0:1, CTA:2 * CTA], KV0[:, 65 * h + 64:65 * h + 65],
                         EQF[:, 0:CTA], start=True, stop=False)
                    t_ab2 = P.mm(BK[ab].pb[0:1, CTA:2 * CTA],
                                 KV0[:, 65 * (16 + h) + 64:65 * (16 + h) + 65],
                                 EQF[:, CTA:2 * CTA], start=False, stop=True, inc=te)
                    BK[ab].wrote(t_ab2)
                    eqf_free = t_ab2
                    t_evA = P.act(NUMF[r0:r0 + 64, hp * CTA:(hp + 1) * CTA],
                                  BK[ab].pb[r0:r0 + 64, 0:CTA], AF.Identity,
                                  wait=[t_ab2], inc=ac)
                    t_evB = P.cp("vector", BSTG,
                                 BK[ab].pb[0:1, CTA:2 * CTA],
                                 wait=[t_ab2, t_evA], inc=ve)
                    BK[ab].read(t_evA)
                    BK[ab].read(t_evB)
                    for tl in range(NTIL_A):
                        t_tpb = P.tp(BK[5].pb[0:128, 32 + tl * 2:33 + tl * 2],
                                     STA[0:1, 384 + tl * 128:384 + (tl + 1) * 128],
                                     ONES[0:1, 0:128], 1,
                                     wait=[t_evB] + BK[5].wwait(), inc=te)
                        BK[5].wrote(t_tpb)
                        t_btc = P.cp("vector", BTT[:, 16 * tl + h:16 * tl + h + 1],
                                     BK[5].pb[0:128, 32 + tl * 2:33 + tl * 2],
                                     wait=[t_tpb], inc=ve)
                        BK[5].read(t_btc)
                t_evs = [(ac, ac.n), (ve, ve.n)]
                t_rd = None
                for tl in range(NTIL_A):
                    P.tt(DENT, EWTM[:, 16 * tl:16 * tl + 16], EHB, ALU.mult,
                         wait=t_evs + [t_eh, t_ewr])
                    P.tt(DENT, DENT, BTT[:, 16 * tl:16 * tl + 16], ALU.add)
                    t_dn = P.recip(DENT, DENT, inc=ve)
                    t_tpr = P.tp(BK[5].pb[0:16, 128:256], DENT, IDENT, 128,
                                 wait=[t_dn] + BK[5].wwait(), inc=te)
                    BK[5].wrote(t_tpr)
                    t_rd = P.cp("vector", RDEN[0:16, tl * 128:(tl + 1) * 128],
                                BK[5].pb[0:16, 128:256], wait=[t_tpr], inc=ve)
                    BK[5].read(t_rd)
                for dc in range(8):
                    bb = 6 + (dc % 2)
                    t_bc = P.mm(BK[bb].pb[0:128, 0:CTA], INDB[0:16, dc * 128:(dc + 1) * 128],
                                RDEN, wait=[t_rd, t_indb] + BK[bb].wwait(), inc=te)
                    BK[bb].wrote(t_bc)
                    t_sc = P.tt(NUMF[:, dc * CTA:(dc + 1) * CTA],
                                NUMF[:, dc * CTA:(dc + 1) * CTA],
                                BK[bb].pb[0:128, 0:CTA], ALU.mult, wait=[t_bc], inc=ve)
                    BK[bb].read(t_sc)
                t_num = (ve, ve.n)
                for oc in range(8):
                    sl = oc % 2
                    w_ap = bass.AP(tensor=wout0.tensor, offset=oc * 128,
                                   ap=[[D, 128], [128 * D, 8], [1, 128]])
                    t_wo = P.dma("gpsimd",
                                 wo_slot[sl].rearrange("p (c j) -> p c j", j=128),
                                 w_ap, ld2, wait=[wo_last[sl]] if wo_last[sl] else [])
                    bk = BK[sl]
                    for dc in range(8):
                        w0 = ([t_wo, t_num] + bk.wwait()) if dc == 0 else []
                        t_o = P.mm(bk.pb[0:128, 0:CTA],
                                   wo_slot[sl][:, dc * 128:(dc + 1) * 128],
                                   NUMF[:, dc * CTA:(dc + 1) * CTA],
                                   start=(dc == 0), stop=(dc == 7), wait=w0,
                                   inc=te if dc == 7 else None)
                    bk.wrote(t_o)
                    t_ra = P.tt(RES[:, oc * T + cols.start:oc * T + cols.stop],
                                RES[:, oc * T + cols.start:oc * T + cols.stop],
                                bk.pb[0:128, 0:CTA], ALU.add, wait=[t_o], inc=ve)
                    bk.read(t_ra)
                    wo_last[sl] = t_ra

        resid_ready = (ve, ve.n)

    if "l0f" in blocks:
        CTF = min(256, T)
        NCHF = T // CTF
        NFB = 32

        P.phase("l0fp")
        t_b1 = P.dma("sync", B1S, bass.AP(tensor=b1.tensor, offset=0,
                                          ap=[[1, 128], [128, 32]]), ld, slow=True)
        t_b2 = P.dma("sync", B2S, bass.AP(tensor=b2.tensor, offset=0,
                                          ap=[[1, 128], [128, 8]]), ld, slow=True)
        wsl = [WS[:, 0:1024], WS[:, 1024:2048]]
        QS = WS[:, 2048:3072]
        TSG = [WS[:, 3072:3456], WS[:, 3456:3840]]

        def w_chunk_ap(w, rc, rowlen):
            r, cb = rc // (rowlen // 1024), rc % (rowlen // 1024)
            return bass.AP(tensor=w.tensor, offset=r * 128 * rowlen + cb * 1024,
                           ap=[[rowlen, 128], [1, 1024]])

        def gamma_of(w, rowlen, nchunk, rg_out):
            fence = [(ve, ve.n), (ac, ac.n), (te, te.n)]
            prev = [fence[0], fence[0]]
            P.ph["lastwait"]
            prev = [None, None]
            first_extra = fence
            t_ab = None
            for rc in range(nchunk):
                sl = rc % 2
                t_w = P.dma("sync", wsl[sl], w_chunk_ap(w, rc, rowlen), ld,
                            wait=([prev[sl]] if prev[sl] else first_extra))
                t_ab = P.act(QS, wsl[sl], AF.Abs, accum_out=GAB[:, rc % 32:rc % 32 + 1],
                             wait=[t_w], inc=ac)
                prev[sl] = t_ab
            t_gs = P.mm(BK[5].pb[0:1, 0:32], ONES[:, 0:1], GAB,
                        wait=[t_ab] + BK[5].wwait(), inc=te)
            BK[5].wrote(t_gs)
            t_red = P.red(LNR[0:1, 0:1], BK[5].pb[0:1, 0:32], ALU.add, wait=[t_gs], inc=ve)
            BK[5].read(t_red)
            P.ts(LNR[0:1, 0:1], LNR[0:1, 0:1], float(1.0 / 4194304.0), 1e-5,
                 ALU.mult, ALU.add)
            t_g = P.recip(LNR[0:1, 1:2], LNR[0:1, 0:1], inc=ve)
            t_gb = P.mm(BK[5].pb[0:128, 32:33], ONES[0:1, 0:128], LNR[0:1, 1:2],
                        wait=[t_g] + BK[5].wwait(), inc=te)
            BK[5].wrote(t_gb)
            t_rg = P.cp("vector", rg_out, BK[5].pb[0:128, 32:33], wait=[t_gb], inc=ve)
            BK[5].read(t_rg)
            return t_rg

        def quant_store(w, rowlen, nchunk, rg, sc, dst_col):
            fence = [(ve, ve.n), (ac, ac.n), (te, te.n)]
            prev = [None, None]
            tsg_free = [None, None]
            t_done = None
            for rc in range(nchunk):
                sl = rc % 2
                t_w = P.dma("sync", wsl[sl], w_chunk_ap(w, rc, rowlen), ld,
                            wait=([prev[sl]] if prev[sl] else fence))
                P.ts(QS, wsl[sl], rg, None, ALU.mult, wait=[t_w, (te, te.n)])
                P.ts(XNKQ[:, 0:1024], QS, C32, C32, ALU.add, ALU.subtract)
                P.ts(QS, QS, C64, C64, ALU.add, ALU.subtract)
                P.stt(QS, XNKQ[:, 0:1024], GAM, QS, ALU.mult, ALU.add)
                t_q = P.ts(QS, QS, 1.0, -1.0, ALU.min, ALU.max, inc=ve)
                prev[sl] = t_q
                for blk in range(8):
                    tb = blk % 2
                    t_tp = P.tp(BK[tb].pb[0:128, 0:128], QS[:, blk * 128:(blk + 1) * 128],
                                IDENT, 128, wait=[t_q] + BK[tb].wwait(), inc=te)
                    BK[tb].wrote(t_tp)
                    t_ev = P.act(TSG[tb][:, 0:128], BK[tb].pb[0:128, 0:128], AF.Identity,
                                 wait=[t_tp] + ([tsg_free[tb]] if tsg_free[tb] else []),
                                 inc=ac)
                    BK[tb].read(t_ev)
                    t_done = P.dma("gpsimd", sc[:, dst_col(rc, blk):dst_col(rc, blk) + 128],
                                   TSG[tb][:, 0:128], st, wait=[t_ev])
                    tsg_free[tb] = t_done
            return t_done

        t_rg1 = gamma_of(w1, 1024, 32, RG1)
        t_rg2 = gamma_of(w2, 4096, 32, RG2)
        t_s1 = quant_store(w1, 1024, 32, RG1, wq1t_sc,
                           lambda rc, blk: rc * 1024 + blk * 128)
        t_s2 = quant_store(w2, 4096, 32, RG2, wq2t_sc,
                           lambda rc, blk: ((rc % 4) * 8 + blk) * 1024 + (rc // 4) * 128)

        P.phase("l0f")
        XNF = XNKQ[:, 0:8 * CTF]
        HF = [UA[:, 0:CTF], UA[:, CTF:2 * CTF]]
        w1sl = [WS[:, 0:1024], WS[:, 1024:2048]]
        w2sl = [WS[:, 2048:3072], WS[:, 3072:4096]]
        w1_prev = [t_s1, t_s1]
        w2_prev = [t_s2, t_s2]
        hf_free = [None, None]
        for ch in range(NCHF):
            cols = slice(ch * CTF, (ch + 1) * CTF)
            t_xn = ln_chunk(cols, XNF, 6, 6, resid_ready)
            t_oacc = None
            for fb in range(NFB):
                sl = fb % 2
                t_w1 = P.dma("sync", w1sl[sl], wq1t_sc[:, fb * 1024:(fb + 1) * 1024],
                             ld, wait=[w1_prev[sl]] if w1_prev[sl] else [])
                t_w2 = P.dma("gpsimd", w2sl[sl], wq2t_sc[:, fb * 1024:(fb + 1) * 1024],
                             ld2, wait=[w2_prev[sl]] if w2_prev[sl] else [])
                hb = BK[fb % 2]
                for dc in range(8):
                    w0 = ([t_w1, t_xn] + hb.wwait()) if dc == 0 else []
                    t_h = P.mm(hb.pb[0:128, 0:CTF],
                               w1sl[sl][:, dc * 128:(dc + 1) * 128],
                               XNF[:, dc * CTF:(dc + 1) * CTF],
                               start=(dc == 0), stop=(dc == 7), wait=w0,
                               inc=te if dc == 7 else None)
                hb.wrote(t_h)
                w1_prev[sl] = t_h
                t_hf = P.act(HF[fb % 2], hb.pb[0:128, 0:CTF], AF.Relu,
                             bias=B1S[:, fb:fb + 1], scale=1.0,
                             wait=[t_h, t_b1] + ([hf_free[fb % 2]] if hf_free[fb % 2] else []),
                             inc=ac)
                hb.read(t_hf)
                for oc in range(8):
                    ob = BK[2 + oc // 2]
                    w0 = [t_hf, t_w2]
                    if fb == 0 and oc % 2 == 0:
                        w0 = w0 + ob.wwait()
                    t_oacc = P.mm(ob.pb[0:128, (oc % 2) * CTF:(oc % 2 + 1) * CTF],
                                  w2sl[sl][:, oc * 128:(oc + 1) * 128], HF[fb % 2],
                                  start=(fb == 0 and oc % 2 == 0),
                                  stop=(fb == NFB - 1 and oc % 2 == 1),
                                  wait=w0, inc=te if (oc == 7) else None)
                    if fb == NFB - 1 and oc % 2 == 1:
                        ob.wrote(t_oacc)
                hf_free[fb % 2] = t_oacc
                w2_prev[sl] = t_oacc
            for oc in range(8):
                ob = BK[2 + oc // 2]
                t_ra = P.stt(RES[:, oc * T + cols.start:oc * T + cols.stop],
                             ob.pb[0:128, (oc % 2) * CTF:(oc % 2 + 1) * CTF],
                             B2S[:, oc:oc + 1],
                             RES[:, oc * T + cols.start:oc * T + cols.stop],
                             ALU.add, ALU.add, wait=[t_oacc, t_b2], inc=ve)
                ob.read(t_ra)
        resid_ready = (ve, ve.n)

    if "l1a" in blocks:
        NSEQ = 2 * T

        P.phase("l1p")
        t_qb = P.dma("sync", QBS, bass.AP(tensor=qkvb1.tensor, offset=0,
                                          ap=[[1, 128], [128, 24]]), ld, slow=True)
        t_pb = P.dma("sync", PJBS, bass.AP(tensor=projb1.tensor, offset=0,
                                           ap=[[1, 128], [128, 8]]), ld, slow=True)
        t_qbv = P.dma("sync", QBV, bass.AP(tensor=qkvb1.tensor, offset=2048,
                                           ap=[[1024, 1], [1, 1024]]), ld)
        t_ih = P.dma("sync", INDH[0:16, :], indh_h[:, :], ld)
        wslp = [WS[:, 0:1024], WS[:, 1024:2048]]
        TSG2 = [WS[:, 2048:2176], WS[:, 2176:2304]]

        def transp_store(w, nchunk, rowlen, sc, dst_col):
            fence = [(ve, ve.n), (ac, ac.n), (te, te.n)]
            prev = [None, None]
            tsg_free = [None, None]
            t_done = None
            for rc in range(nchunk):
                sl = rc % 2
                t_w = P.dma("sync", wslp[sl], bass.AP(tensor=w.tensor,
                                                      offset=rc * 128 * rowlen,
                                                      ap=[[rowlen, 128], [1, 1024]]),
                            ld, wait=([prev[sl]] if prev[sl] else fence))
                for blk in range(8):
                    tb = blk % 2
                    t_tp = P.tp(BK[tb].pb[0:128, 0:128],
                                wslp[sl][:, blk * 128:(blk + 1) * 128],
                                IDENT, 128, wait=[t_w] + BK[tb].wwait(), inc=te)
                    BK[tb].wrote(t_tp)
                    prev[sl] = t_tp
                    t_ev = P.act(TSG2[tb], BK[tb].pb[0:128, 0:128], AF.Identity,
                                 wait=[t_tp] + ([tsg_free[tb]] if tsg_free[tb] else []),
                                 inc=ac)
                    BK[tb].read(t_ev)
                    t_done = P.dma("gpsimd", sc[:, dst_col(rc, blk):dst_col(rc, blk) + 128],
                                   TSG2[tb], st, wait=[t_ev])
                    tsg_free[tb] = t_done
            return t_done

        t_tq = transp_store(qkvw1, 24, 1024, qkv1t_sc,
                            lambda rc, blk: blk * 3072 + rc * 128)
        t_tp2 = transp_store(projw1, 8, 1024, proj1t_sc,
                             lambda rc, blk: blk * 1024 + rc * 128)
        t_pm1 = P.dma("sync", WS[0:64, 2304:2368], pm1[:, :], ld,
                      wait=[(te, te.n), (ac, ac.n)])
        t_tpm1 = P.tp(BK[0].pb[0:64, 0:64], WS[0:64, 2304:2368], IDENT, 64,
                      wait=[t_pm1] + BK[0].wwait(), inc=te)
        BK[0].wrote(t_tpm1)
        t_pm1a = P.act(PMT1[0:64, :], BK[0].pb[0:64, 0:64], AF.Identity, scale=0.125,
                       wait=[t_tpm1], inc=ac)
        BK[0].read(t_pm1a)
        t_pm1b = P.dma("gpsimd", PMT1[64:128, :], PMT1[0:64, :], st, wait=[t_pm1a])

        P.phase("l1a")
        P.memset(bass.AP(tensor=KV0, offset=64, ap=[[2080, 64], [65, 16]]), 0.0, inc=gp)
        P.memset(XSUM, 0.0, inc=gp)
        t_init1 = (gp, gp.n)
        kv_open1 = [False, False]
        ksr_open = [False, False]
        t_kv_last1 = [None, None]
        t_ksr_last = [None, None]
        ekt_free = [None, None]
        for ch in range(NCH_A):
            cols = slice(ch * CTA, (ch + 1) * CTA)
            t_xn = ln_chunk(cols, XN, 5, 5, resid_ready)
            for kc in range(8):
                w_ap = bass.AP(tensor=qkv1t_sc, offset=(D + kc * 128),
                               ap=[[8 * 3072, 128], [3072, 8], [1, 128]])
                t_wk = P.dma("sync", WS[:, 0:1024].rearrange("p (c j) -> p c j", j=128),
                             w_ap, ld, wait=[(ac, ac.n), (te, te.n)] if kc == 0 else [(te, te.n)])
                bk = BK[0]
                for dc in range(8):
                    w0 = ([t_wk, t_xn, t_tq] + bk.wwait()) if dc == 0 else []
                    t_kmm = P.mm(bk.pb[0:128, 0:CTA],
                                 WS[:, dc * 128:(dc + 1) * 128],
                                 XN[:, dc * CTA:(dc + 1) * CTA],
                                 start=(dc == 0), stop=(dc == 7), wait=w0, inc=te)
                bk.wrote(t_kmm)
                t_kev = P.act(KQ[:, kc * CTA:(kc + 1) * CTA], bk.pb[0:128, 0:CTA],
                              AF.Identity, bias=QBS[:, 8 + kc:9 + kc], scale=1.0,
                              wait=[t_kmm, t_qb], inc=ac)
                bk.read(t_kev)
            t_k = (ac, ac.n)
            for vg in range(4):
                w_ap = bass.AP(tensor=qkv1t_sc, offset=(2 * D + vg * 256),
                               ap=[[8 * 3072, 128], [3072, 8], [1, 256]])
                t_wv = P.dma("gpsimd",
                             WS[:, 2048:4096].rearrange("p (c j) -> p c j", j=256),
                             w_ap, ld2, wait=[(te, te.n)])
                for tl in range(NTIL_A):
                    bk = BK[0]
                    for dc in range(8):
                        w0 = ([t_wv, t_xn] + bk.wwait()) if dc == 0 else []
                        t_vmm = P.mm(bk.pb[0:128, 0:256],
                                     XN[:, dc * CTA + tl * 128: dc * CTA + (tl + 1) * 128],
                                     WS[:, 2048 + dc * 256:2048 + (dc + 1) * 256],
                                     start=(dc == 0), stop=(dc == 7), wait=w0,
                                     inc=te if dc == 7 else None)
                    bk.wrote(t_vmm)
                    t_vev = P.act(VT[:, tl * 1024 + vg * 256: tl * 1024 + (vg + 1) * 256],
                                  bk.pb[0:128, 0:256], AF.Identity, wait=[t_vmm], inc=ac)
                    bk.read(t_vev)
            t_v = (ac, ac.n)
            for dc in range(8):
                P.red(SCRR, XN[:, dc * CTA:(dc + 1) * CTA], ALU.add, wait=[t_xn])
                P.tt(XSUM[:, dc:dc + 1], XSUM[:, dc:dc + 1], SCRR, ALU.add,
                     wait=[t_init1], inc=ve)
            for tl in range(NTIL_A):
                for h in range(H):
                    hp, r0 = h // 2, 64 * (h % 2)
                    ub = 2 + (h % 2)
                    t_g = P.mm(BK[ub].pb[0:128, 0:64],
                               KQ[r0:r0 + 64, hp * CTA + tl * 128: hp * CTA + (tl + 1) * 128],
                               PMT1[r0:r0 + 64, 0:64],
                               wait=[t_k, t_v, t_pm1b] + BK[ub].wwait(), inc=te)
                    BK[ub].wrote(t_g)
                    t_rm = P.red(SCRR, BK[ub].pb[0:128, 0:64], ALU.max, wait=[t_g], inc=ve)
                    BK[ub].read(t_rm)
                    t_nr = P.ts(SCR1[:, 0:1], SCRR, -1.0, float(-np.log(8.0)),
                                ALU.mult, ALU.add, inc=ve)
                    es = h % 2
                    t_ek = P.act(EKT[:, es * 64:es * 64 + 64], BK[ub].pb[0:128, 0:64],
                                 AF.Exp, bias=SCR1[:, 0:1], scale=1.0,
                                 wait=[t_nr, t_g] + ([ekt_free[es]] if ekt_free[es] else []),
                                 inc=ac)
                    BK[ub].read(t_ek)
                    kb = 1 if h < 8 else 4
                    kbi = 0 if h < 8 else 1
                    stt0 = not kv_open1[kbi]
                    kv_open1[kbi] = True
                    is_last = (ch == NCH_A - 1 and tl == NTIL_A - 1 and (h % 8) == 7)
                    t_kv = P.mm(PB[kb][0:64, (h % 8) * 64:(h % 8) * 64 + 64],
                                EKT[:, es * 64:es * 64 + 64],
                                VT[:, tl * 1024 + h * 64: tl * 1024 + (h + 1) * 64],
                                start=stt0, stop=is_last, wait=[t_ek], inc=te)
                    t_kv_last1[kbi] = t_kv
                    ekt_free[es] = t_kv
                    t_ks = P.mm(BK[ub].pb[0:64, 256 + (h // 2):257 + (h // 2)],
                                EKT[:, es * 64:es * 64 + 64], ONES[:, 0:1],
                                start=True, stop=True, wait=[t_ek], inc=te)
                    BK[ub].wrote(t_ks)
                    t_ka = P.tt(KV1[0:64, 65 * h + 64:65 * h + 65],
                                KV1[0:64, 65 * h + 64:65 * h + 65],
                                BK[ub].pb[0:64, 256 + (h // 2):257 + (h // 2)],
                                ALU.add, wait=[t_ks, t_init1], inc=ve)
                    BK[ub].read(t_ka)
                    rb = 6 if h < 8 else 7
                    rbi = 0 if h < 8 else 1
                    str0 = not ksr_open[rbi]
                    ksr_open[rbi] = True
                    t_kr = P.mm(PB[rb][0:1, (h % 8) * 64:(h % 8) * 64 + 64],
                                ONES[:, 0:1], EKT[:, es * 64:es * 64 + 64],
                                start=str0, stop=is_last, inc=te)
                    t_ksr_last[rbi] = t_kr
                    ekt_free[es] = t_kr
        t_kvev1 = None
        for i, kb in enumerate((1, 4)):
            out_ap = bass.AP(tensor=KV0, offset=65 * 8 * i,
                             ap=[[2080, 64], [65, 8], [1, 64]])
            in_ap = PB[kb][0:64, 0:512].rearrange("p (b j) -> p b j", j=64)
            t_kvev1 = P.cp("vector", out_ap, in_ap, wait=[t_kv_last1[i]], inc=ve)
            BK[kb].read(t_kvev1)
            BK[kb].wtok = None
        t_rowout = None
        for i, rb in enumerate((6, 7)):
            t_kr2 = P.cp("vector", LNR[0:1, 0:512], PB[rb][0:1, 0:512],
                         wait=[t_ksr_last[i], t_rowout] if t_rowout else [t_ksr_last[i]],
                         inc=ve)
            BK[rb].read(t_kr2)
            BK[rb].wtok = None
            t_rowout = P.dma("gpsimd", row_bi[0:1, i * 512:(i + 1) * 512],
                             LNR[0:1, 0:512], st, wait=[t_kr2])
        for vg in range(4):
            w_ap = bass.AP(tensor=qkv1t_sc, offset=(2 * D + vg * 256),
                           ap=[[8 * 3072, 128], [3072, 8], [1, 256]])
            t_wv2 = P.dma("sync", WS[:, 2048:4096].rearrange("p (c j) -> p c j", j=256),
                          w_ap, ld, wait=[(te, te.n)])
            for dc in range(8):
                w0 = ([t_wv2, (ve, ve.n)] + BK[5].wwait()) if dc == 0 else []
                t_vs = P.mm(BK[5].pb[0:1, (vg % 2) * 256:(vg % 2) * 256 + 256],
                            XSUM[:, dc:dc + 1],
                            WS[:, 2048 + dc * 256:2048 + (dc + 1) * 256],
                            start=(dc == 0), stop=(dc == 7), wait=w0, inc=te)
            BK[5].wrote(t_vs)
            t_ve2 = P.cp("vector", LNR[0:1, (vg % 2) * 256:(vg % 2) * 256 + 256],
                         BK[5].pb[0:1, (vg % 2) * 256:(vg % 2) * 256 + 256],
                         wait=[t_vs, t_rowout], inc=ve)
            BK[5].read(t_ve2)
            if vg % 2 == 1:
                t_rowout = P.dma("gpsimd", row_bi[0:1, 1024 + (vg // 2) * 512:
                                                   1024 + (vg // 2 + 1) * 512],
                                 LNR[0:1, 0:512], st, wait=[t_ve2])
        t_s1 = P.dma("gpsimd", kv1_bi[:, :], KV1[:, :], st, wait=[t_kvev1, (ve, ve.n)])
        pairs_g = [[2 * i, 2 * i + 1] for i in range(n_cores // 2)]
        t_c1 = P.coll("AllReduce", ALU.add, pairs_g, [kv1_bi[:, :].opt()],
                      [kv1_bo[:, :].opt()], wait=[t_s1], inc=cc)
        t_c2 = P.coll("AllReduce", ALU.add, pairs_g, [row_bi[:, :].opt()],
                      [row_bo[:, :].opt()], wait=[t_rowout], inc=cc)
        t_kv1 = P.dma("gpsimd", KV1[:, :], kv1_bo[:, :], st, wait=[t_c1])
        t_ksr = P.dma("gpsimd", KSR, row_bo[0:1, 0:1024], st, wait=[t_c2])
        t_vsf = t_ksr
        for k2 in range(2):
            t_vl = P.dma("sync", LNR[0:1, 0:512], row_bo[0:1, 1024 + k2 * 512:
                                                         1024 + (k2 + 1) * 512],
                         ld, wait=[t_c2, (ve, ve.n), t_vsf])
            t_bv = P.dma("sync", SQS[0:1, 0:512],
                         bass.AP(tensor=qkvb1.tensor, offset=2048 + k2 * 512,
                                 ap=[[512, 1], [1, 512]]), ld, wait=[(ve, ve.n)])
            P.stt(LNR[0:1, 0:512], SQS[0:1, 0:512], float(NSEQ), LNR[0:1, 0:512],
                  ALU.mult, ALU.add, wait=[t_vl, t_bv])
            t_v0 = P.ts(LNR[0:1, 0:512], LNR[0:1, 0:512], EPS0, None, ALU.mult, inc=ve)
            t_vsf = P.dma("gpsimd", KV0[65:66, 1040 + k2 * 512:1040 + (k2 + 1) * 512],
                          LNR[0:1, 0:512], st, wait=[t_v0])
        t_o65 = None
        for k8 in range(8):
            t_o65 = P.dma("gpsimd", ONE65[0:1, k8 * 128:(k8 + 1) * 128],
                          ONES[64:65, 0:128], st, wait=[t_ones])
        t_fix = None
        for h in range(H):
            fb = BK[5]
            t_r1 = P.mm(fb.pb[0:64, 0:64], KV0[64:66, 64 * h:64 * h + 64],
                        KV0[64:66, 1040 + 64 * h:1040 + 64 * h + 64],
                        start=True, stop=True,
                        wait=[t_vsf, t_kv1, t_qbv, t_o65] + fb.wwait(), inc=te)
            fb.wrote(t_r1)
            t_fix = P.tt(KV1[0:64, 65 * h:65 * h + 64], KV1[0:64, 65 * h:65 * h + 64],
                         fb.pb[0:64, 0:64], ALU.add, wait=[t_r1], inc=ve)
            fb.read(t_fix)
        t_kfix = P.ts(bass.AP(tensor=KV0, offset=64, ap=[[2080, 64], [65, 16]]),
                      bass.AP(tensor=KV0, offset=64, ap=[[2080, 64], [65, 16]]),
                      float(EPS0 * NSEQ), None, ALU.add, wait=[t_fix], inc=ve)
        ks_ap = bass.AP(tensor=KV0, offset=64, ap=[[2080, 64], [65, 16]])
        t_tm = P.mm(BK[5].pb[0:16, 0:1], ks_ap, ONES[0:64, 0:1],
                    wait=[t_kfix] + BK[5].wwait(), inc=te)
        BK[5].wrote(t_tm)
        t_tv = P.ts(STA[0:16, 89:90], BK[5].pb[0:16, 0:1], EPS0, 1e-6,
                    ALU.mult, ALU.add, wait=[t_tm], inc=ve)
        BK[5].read(t_tv)
        t_tpt = P.tp(BK[5].pb[0:1, 0:16], STA[0:16, 89:90], IDENT, 16,
                     wait=[t_tv] + BK[5].wwait(), inc=te)
        BK[5].wrote(t_tpt)
        t_tr = P.cp("vector", LNR[0:1, 0:16],
                    BK[5].pb[0:1, 0:16], wait=[t_tpt], inc=ve)
        BK[5].read(t_tr)
        t_etb0 = P.mm(BK[5].pb[0:128, 16:32], ONES[0:1, 0:128], LNR[0:1, 0:16],
                      wait=[t_tr] + BK[5].wwait(), inc=te)
        BK[5].wrote(t_etb0)
        t_etb = P.cp("vector", ETB, BK[5].pb[0:128, 16:32], wait=[t_etb0], inc=ve)
        BK[5].read(t_etb)
        t_esb = None
        for h in range(H):
            t_sm = P.mm(BK[5].pb[0:64, 32 + h:33 + h], KV1[0:64, 65 * h:65 * h + 64],
                        ONES[0:64, 0:1], wait=BK[5].wwait(), inc=te)
            BK[5].wrote(t_sm)
            t_e0 = P.ts(ESB[0:64, h:h + 1], BK[5].pb[0:64, 32 + h:33 + h],
                        EPS0, None, ALU.mult, wait=[t_sm], inc=ve)
            BK[5].read(t_e0)
        t_esb = P.dma("gpsimd", ESB2[64:128, 0:16], ESB[0:64, 0:16], st,
                      wait=[(ve, ve.n)])

        P.phase("l1b")
        eqf_free = None
        prj_last = [None, None]
        for ch in range(NCH_A):
            cols = slice(ch * CTA, (ch + 1) * CTA)
            t_xn = ln_chunk(cols, XN, 5, 5, resid_ready)
            for qc in range(8):
                w_ap = bass.AP(tensor=qkv1t_sc, offset=qc * 128,
                               ap=[[8 * 3072, 128], [3072, 8], [1, 128]])
                t_wq = P.dma("sync", WS[:, 0:1024].rearrange("p (c j) -> p c j", j=128),
                             w_ap, ld, wait=[(ac, ac.n), (te, te.n)])
                bk = BK[0]
                for dc in range(8):
                    w0 = ([t_wq, t_xn] + bk.wwait()) if dc == 0 else []
                    t_qmm = P.mm(bk.pb[0:128, 0:CTA],
                                 WS[:, dc * 128:(dc + 1) * 128],
                                 XN[:, dc * CTA:(dc + 1) * CTA],
                                 start=(dc == 0), stop=(dc == 7), wait=w0, inc=te)
                bk.wrote(t_qmm)
                t_qev = P.act(KQ[:, qc * CTA:(qc + 1) * CTA], bk.pb[0:128, 0:CTA],
                              AF.Identity, bias=QBS[:, qc:qc + 1], scale=1.0,
                              wait=[t_qmm], inc=ac)
                bk.read(t_qev)
            t_q = (ac, ac.n)
            for tl in range(NTIL_A):
                for h in range(H):
                    hp, r0 = h // 2, 64 * (h % 2)
                    ub = 2 + (h % 2)
                    t_g = P.mm(BK[ub].pb[0:128, 0:64],
                               KQ[r0:r0 + 64, hp * CTA + tl * 128: hp * CTA + (tl + 1) * 128],
                               PMT1[r0:r0 + 64, 0:64],
                               wait=[t_q] + BK[ub].wwait(), inc=te)
                    BK[ub].wrote(t_g)
                    t_rm = P.red(MQT[:, h:h + 1], BK[ub].pb[0:128, 0:64], ALU.max,
                                 wait=[t_g], inc=ve)
                    BK[ub].read(t_rm)
                t_ml8 = P.ts(MQT, MQT, -1.0, float(-np.log(8.0)), ALU.mult, ALU.add,
                             inc=ve)
                t_tpr = P.tp(BK[5].pb[0:16, 0:128], MQT, IDENT, 128,
                             wait=[t_ml8] + BK[5].wwait(), inc=te)
                BK[5].wrote(t_tpr)
                t_rmx = P.cp("vector", RMXR[0:16, tl * 128:(tl + 1) * 128],
                             BK[5].pb[0:16, 0:128], wait=[t_tpr], inc=ve)
                BK[5].read(t_rmx)
            t_rmr = (ve, ve.n)
            for h in range(H):
                hp, r0 = h // 2, 64 * (h % 2)
                ub = 2 + (h % 2)
                P.mm(BK[ub].pb[0:64, 0:CTA], PMT1[r0:r0 + 64, 0:64],
                     KQ[r0:r0 + 64, hp * CTA: hp * CTA + CTA],
                     start=True, stop=False,
                     wait=[t_rmr, t_ih] + BK[ub].wwait())
                t_gq = P.mm(BK[ub].pb[0:64, 0:CTA], INDH[0:16, h * 64:(h + 1) * 64],
                            RMXR, start=False, stop=True, inc=te)
                BK[ub].wrote(t_gq)
                t_eq = P.act(EQF[0:64, CTA:2 * CTA], BK[ub].pb[0:64, 0:CTA], AF.Exp,
                             wait=[t_gq] + ([eqf_free] if eqf_free else []), inc=ac)
                BK[ub].read(t_eq)
                ab = 4 + 3 * (h % 2)
                w0 = [t_eq] + BK[ab].wwait()
                P.mm(BK[ab].pb[r0:r0 + 64, 0:CTA], KV1[0:64, 65 * h:65 * h + 64],
                     EQF[0:64, CTA:2 * CTA], start=True, stop=True, wait=w0)
                t_ab2 = P.mm(BK[ab].pb[0:1, CTA:2 * CTA],
                             KV1[0:64, 65 * h + 64:65 * h + 65],
                             EQF[0:64, CTA:2 * CTA], start=True, stop=True, inc=te)
                BK[ab].wrote(t_ab2)
                eqf_free = t_ab2
                esb_t = ESB if r0 == 0 else ESB2
                t_evA = P.act(NUMF[r0:r0 + 64, hp * CTA:(hp + 1) * CTA],
                              BK[ab].pb[r0:r0 + 64, 0:CTA], AF.Identity,
                              bias=esb_t[r0:r0 + 64, h:h + 1], scale=1.0,
                              wait=[t_ab2, t_esb], inc=ac)
                t_evB = P.cp("vector", BSTG,
                             BK[ab].pb[0:1, CTA:2 * CTA], wait=[t_ab2, t_evA], inc=ve)
                BK[ab].read(t_evA)
                BK[ab].read(t_evB)
                for tl in range(NTIL_A):
                    t_tpb = P.tp(BK[5].pb[0:128, 32 + tl * 2:33 + tl * 2],
                                 STA[0:1, 384 + tl * 128:384 + (tl + 1) * 128],
                                 ONES[0:1, 0:128], 1,
                                 wait=[t_evB] + BK[5].wwait(), inc=te)
                    BK[5].wrote(t_tpb)
                    t_btc = P.cp("vector", BTT[:, 16 * tl + h:16 * tl + h + 1],
                                 BK[5].pb[0:128, 32 + tl * 2:33 + tl * 2],
                                 wait=[t_tpb], inc=ve)
                    BK[5].read(t_btc)
            t_evs = [(ac, ac.n), (ve, ve.n)]
            t_rd = None
            for tl in range(NTIL_A):
                P.tt(DENT, BTT[:, 16 * tl:16 * tl + 16], ETB, ALU.add,
                     wait=t_evs + [t_etb])
                t_dn = P.recip(DENT, DENT, inc=ve)
                t_tpr = P.tp(BK[5].pb[0:16, 128:256], DENT, IDENT, 128,
                             wait=[t_dn] + BK[5].wwait(), inc=te)
                BK[5].wrote(t_tpr)
                t_rd = P.cp("vector", RDEN[0:16, tl * 128:(tl + 1) * 128],
                            BK[5].pb[0:16, 128:256], wait=[t_tpr], inc=ve)
                BK[5].read(t_rd)
            for dc in range(8):
                bb = 6 + (dc % 2)
                t_bcr = P.mm(BK[bb].pb[0:128, 0:CTA], INDB[0:16, dc * 128:(dc + 1) * 128],
                             RDEN, wait=[t_rd, t_indb] + BK[bb].wwait(), inc=te)
                BK[bb].wrote(t_bcr)
                t_sc = P.tt(NUMF[:, dc * CTA:(dc + 1) * CTA],
                            NUMF[:, dc * CTA:(dc + 1) * CTA],
                            BK[bb].pb[0:128, 0:CTA], ALU.mult, wait=[t_bcr], inc=ve)
                BK[bb].read(t_sc)
            t_num = (ve, ve.n)
            for oc in range(8):
                sl = oc % 2
                w_ap = bass.AP(tensor=proj1t_sc, offset=oc * 128,
                               ap=[[8 * 1024, 128], [1024, 8], [1, 128]])
                t_wo = P.dma("gpsimd",
                             WS[:, 2048 + sl * 1024:2048 + (sl + 1) * 1024].rearrange(
                                 "p (c j) -> p c j", j=128),
                             w_ap, ld2, wait=[prj_last[sl]] if prj_last[sl] else [(te, te.n)])
                bk = BK[sl]
                for dc in range(8):
                    w0 = ([t_wo, t_num, t_tp2] + bk.wwait()) if dc == 0 else []
                    t_o = P.mm(bk.pb[0:128, 0:CTA],
                               WS[:, 2048 + sl * 1024 + dc * 128:2048 + sl * 1024 + (dc + 1) * 128],
                               NUMF[:, dc * CTA:(dc + 1) * CTA],
                               start=(dc == 0), stop=(dc == 7), wait=w0, inc=te)
                bk.wrote(t_o)
                t_ra = P.stt(RES[:, oc * T + cols.start:oc * T + cols.stop],
                             bk.pb[0:128, 0:CTA], PJBS[:, oc:oc + 1],
                             RES[:, oc * T + cols.start:oc * T + cols.stop],
                             ALU.add, ALU.add, wait=[t_o, t_pb], inc=ve)
                bk.read(t_ra)
                prj_last[sl] = t_ra
        resid_ready = (ve, ve.n)

    if "l1f" in blocks:
        CT1 = min(256, T)
        NCH1 = T // CT1
        NFB1 = 24
        NST = (NFB1 * T) // 512

        S1B = STA[:, 352:353]
        RS1B = STA[:, 353:354]
        S2B = STA[:, 354:355]
        RS2B = STA[:, 355:356]
        S3B = STA[:, 356:357]
        RS3B = STA[:, 357:358]
        K3B = STA[:, 358:359]
        S4B = STA[:, 359:360]
        RS4B = STA[:, 360:361]
        C34B = STA[:, 361:362]
        K45B = STA[:, 362:363]
        S5B = STA[:, 363:364]
        C45B = STA[:, 364:365]
        R327 = STA[:, 365:366]
        R7 = STA[:, 366:367]
        HALF = STA[:, 367:368]
        MXP = STA[:, 368:369]
        MBC = STA[:, 369:370]

        P.phase("l1fp")
        P.memset(R327, 32767.0, inc=gp)
        P.memset(R7, 7.0, inc=gp)
        P.memset(HALF, 0.5, inc=gp)
        t_cst = (gp, gp.n)
        P.recip(R327, R327, wait=[t_cst])
        t_r7 = P.recip(R7, R7, inc=ve)
        wslp2 = [WS[:, 0:1024], WS[:, 1024:2048]]
        QS2 = WS[:, 2048:3072]
        TSG3 = [WS[:, 3072:3200], WS[:, 3200:3328]]
        SEG = GAB
        RSEG = BTT[:, 0:32]

        def lpbq_store(w, nchunk, rowlen, chunk_ap, sc, dst_col):
            fence = [(ve, ve.n), (ac, ac.n), (te, te.n)]
            prev = [None, None]
            tsg_free = [None, None]
            t_done = None
            for rc in range(nchunk):
                sl = rc % 2
                t_w = P.dma("sync", wslp2[sl], chunk_ap(rc), ld,
                            wait=([prev[sl]] if prev[sl] else fence))
                t_sg = P.red(SEG, wslp2[sl].rearrange("p (s w) -> p s w", w=32),
                             ALU.max, absval=True, wait=[t_w, t_r7, (te, te.n)], inc=ve)
                P.op("vector", lambda e, sl=sl: e.tensor_scalar(
                    SEG, SEG, R7, 1e-12, ALU.mult, ALU.max))
                P.recip(RSEG, SEG)
                for j in range(32):
                    P.op("vector", lambda e, sl=sl, j=j: e.tensor_scalar(
                        QS2[:, j * 32:(j + 1) * 32], wslp2[sl][:, j * 32:(j + 1) * 32],
                        RSEG[:, j:j + 1], None, ALU.mult))
                P.ts(XNKQ[:, 0:1024], QS2, C32, C32, ALU.add, ALU.subtract)
                P.ts(QS2, QS2, C64, C64, ALU.add, ALU.subtract)
                P.stt(QS2, XNKQ[:, 0:1024], GAM, QS2, ALU.mult, ALU.add)
                t_q = None
                for j in range(32):
                    t_q = P.op("vector", lambda e, j=j: e.tensor_scalar(
                        QS2[:, j * 32:(j + 1) * 32], QS2[:, j * 32:(j + 1) * 32],
                        SEG[:, j:j + 1], None, ALU.mult), inc=ve)
                prev[sl] = t_q
                for blk in range(8):
                    tb = blk % 2
                    t_tp = P.tp(BK[tb].pb[0:128, 0:128], QS2[:, blk * 128:(blk + 1) * 128],
                                IDENT, 128, wait=[t_q] + BK[tb].wwait(), inc=te)
                    BK[tb].wrote(t_tp)
                    t_ev = P.act(TSG3[tb], BK[tb].pb[0:128, 0:128], AF.Identity,
                                 wait=[t_tp] + ([tsg_free[tb]] if tsg_free[tb] else []),
                                 inc=ac)
                    BK[tb].read(t_ev)
                    t_done = P.dma("gpsimd", sc[:, dst_col(rc, blk):dst_col(rc, blk) + 128],
                                   TSG3[tb], st, wait=[t_ev])
                    tsg_free[tb] = t_done
            return t_done

        def rows_ap(w, rowlen):
            return lambda rc: bass.AP(tensor=w.tensor, offset=rc * 128 * rowlen,
                                      ap=[[rowlen, 128], [1, 1024]])

        def wd_ap(rc):
            r, cb = rc // 3, rc % 3
            return bass.AP(tensor=wd1.tensor, offset=r * 128 * 3072 + cb * 1024,
                           ap=[[3072, 128], [1, 1024]])

        t_wu = lpbq_store(wu1, 24, 1024, rows_ap(wu1, 1024), wut_sc,
                          lambda rc, blk: rc * 1024 + blk * 128)
        t_wg = lpbq_store(wg1, 24, 1024, rows_ap(wg1, 1024), wgt_sc,
                          lambda rc, blk: rc * 1024 + blk * 128)
        t_wd = lpbq_store(wd1, 24, 3072, wd_ap, wdt_sc,
                          lambda rc, blk: ((rc % 3) * 8 + blk) * 1024 + (rc // 3) * 128)

        def global_max(nvals, stage_tok, vals_fn):
            t_st = P.dma("gpsimd", mx_bi[0:1, 0:nvals], LNR[0:1, 32:32 + nvals], st,
                         wait=stage_tok)
            t_cc = P.coll("AllReduce", ALU.max, [list(range(n_cores))],
                          [mx_bi[0:1, 0:nvals].opt()], [mx_bo[0:1, 0:nvals].opt()],
                          wait=[t_st], inc=cc)
            return P.dma("gpsimd", LNR[0:1, 0:nvals], mx_bo[0:1, 0:nvals], st,
                         wait=[t_cc])

        def finish_partial(col):
            t_tp = P.tp(BK[5].pb[0:1, 0:128], MXP, IDENT, 128,
                        wait=[(ve, ve.n)] + BK[5].wwait(), inc=te)
            BK[5].wrote(t_tp)
            t_c = P.cp("vector", LNR[0:1, 128:256], BK[5].pb[0:1, 0:128],
                       wait=[t_tp], inc=ve)
            BK[5].read(t_c)
            return P.red(LNR[0:1, 32 + col:33 + col], LNR[0:1, 128:256], ALU.max,
                         wait=[t_c], inc=ve)

        def bcast_scalar(src_col, dst, wait):
            t_b = P.mm(BK[5].pb[0:128, 200:201], ONES[0:1, 0:128],
                       LNR[0:1, src_col:src_col + 1],
                       wait=wait + BK[5].wwait(), inc=te)
            BK[5].wrote(t_b)
            t_r = P.cp("vector", dst, BK[5].pb[0:128, 200:201], wait=[t_b], inc=ve)
            BK[5].read(t_r)
            return t_r

        P.phase("l1f0")
        P.memset(MXP, 0.0, inc=gp)
        t_mx0 = (gp, gp.n)
        XNF1 = XNKQ[:, 0:8 * CT1]
        for ch in range(NCH1):
            cols = slice(ch * CT1, (ch + 1) * CT1)
            t_xn = ln_chunk(cols, XNF1, 5, 5, resid_ready)
            t_r = P.red(SCRR, XNF1, ALU.max, absval=True, wait=[t_xn], inc=ve)
            P.tt(MXP, MXP, SCRR, ALU.max, wait=[t_mx0], inc=ve)
        t_p = finish_partial(0)
        t_m1 = global_max(1, [t_p], None)
        t_mb = bcast_scalar(0, MBC, [t_m1])
        P.op("vector", lambda e: e.tensor_scalar(S1B, MBC, R327, 1e-12, ALU.mult, ALU.max),
             wait=[t_mb])
        t_s1b = P.recip(RS1B, S1B, inc=ve)

        P.phase("l1f1")
        P.memset(MXP, 0.0)
        P.memset(STA[:, 370:371], 0.0, inc=gp)
        t_mx1 = (gp, gp.n)
        MXP2 = STA[:, 370:371]
        upg_free = [None, None]
        wu_prev = [t_wu, t_wu]
        wg_prev = [t_wg, t_wg]
        for ch in range(NCH1):
            cols = slice(ch * CT1, (ch + 1) * CT1)
            t_xn = ln_chunk(cols, XNF1, 5, 5, resid_ready)
            t_qx = None
            npc = (8 * CT1) // 512
            for pc in range(npc):
                seg = XNF1[:, pc * 512:(pc + 1) * 512]
                t_w = P.act(UA[:, 0:512], seg, AF.Identity, scale=RS1B,
                            wait=[t_xn, t_s1b, (ve, ve.n)], inc=ac)
                P.ts(SQS[:, 0:512], UA[:, 0:512], C32, C32, ALU.add, ALU.subtract,
                     wait=[t_w])
                P.ts(UA[:, 512:1024], UA[:, 0:512], C64, C64, ALU.add, ALU.subtract)
                t_qx = P.stt(seg, SQS[:, 0:512], GAM, UA[:, 512:1024],
                             ALU.mult, ALU.add, inc=ve)
            for fb in range(NFB1):
                sl = fb % 2
                t_w1 = P.dma("sync", WS[:, sl * 1024:(sl + 1) * 1024],
                             wut_sc[:, fb * 1024:(fb + 1) * 1024], ld,
                             wait=[wu_prev[sl]] if wu_prev[sl] else [])
                t_w2 = P.dma("gpsimd", WS[:, 2048 + sl * 1024:2048 + (sl + 1) * 1024],
                             wgt_sc[:, fb * 1024:(fb + 1) * 1024], ld2,
                             wait=[wg_prev[sl]] if wg_prev[sl] else [])
                for which, t_wx, wofs, scx, mxacc, spill in (
                        (0, t_w1, sl * 1024, S1B, MXP, upre_sc),
                        (1, t_w2, 2048 + sl * 1024, S1B, MXP2, gpre_sc)):
                    bk = BK[which]
                    for dc in range(8):
                        w0 = ([t_wx, t_qx] + bk.wwait()) if dc == 0 else []
                        t_u = P.mm(bk.pb[0:128, 0:CT1],
                                   WS[:, wofs + dc * 128:wofs + (dc + 1) * 128],
                                   XNF1[:, dc * CT1:(dc + 1) * CT1],
                                   start=(dc == 0), stop=(dc == 7), wait=w0, inc=te)
                    bk.wrote(t_u)
                    slot = UA[:, 1024 + which * 512:1024 + which * 512 + CT1]
                    t_ev = P.act(slot, bk.pb[0:128, 0:CT1], AF.Identity, scale=scx,
                                 wait=[t_u] + ([upg_free[which]] if upg_free[which] else []),
                                 inc=ac)
                    bk.read(t_ev)
                    t_rd = P.red(SCRR, slot, ALU.max, absval=True, wait=[t_ev], inc=ve)
                    P.tt(mxacc, mxacc, SCRR, ALU.max, wait=[t_mx1], inc=ve)
                    t_sp = P.dma("gpsimd", spill[:, fb * T + cols.start:fb * T + cols.stop],
                                 slot, st, wait=[t_ev, t_rd])
                    upg_free[which] = t_sp
                    if which == 0:
                        wu_prev[sl] = t_u
                    else:
                        wg_prev[sl] = t_u
        t_p2 = finish_partial(0)
        t_tp = P.tp(BK[5].pb[0:1, 0:128], MXP2, IDENT, 128,
                    wait=[(ve, ve.n)] + BK[5].wwait(), inc=te)
        BK[5].wrote(t_tp)
        t_c = P.cp("vector", LNR[0:1, 128:256], BK[5].pb[0:1, 0:128], wait=[t_tp], inc=ve)
        BK[5].read(t_c)
        t_p3 = P.red(LNR[0:1, 33:34], LNR[0:1, 128:256], ALU.max, wait=[t_c], inc=ve)
        t_m23 = global_max(2, [t_p2, t_p3], None)
        t_mb2 = bcast_scalar(0, MBC, [t_m23])
        P.op("vector", lambda e: e.tensor_scalar(S2B, MBC, R327, 1e-12, ALU.mult, ALU.max),
             wait=[t_mb2])
        P.recip(RS2B, S2B)
        t_mb3 = bcast_scalar(1, MBC, [t_m23])
        P.op("vector", lambda e: e.tensor_scalar(S3B, MBC, R327, 1e-12, ALU.mult, ALU.max),
             wait=[t_mb3])
        P.recip(RS3B, S3B)
        t_k3 = P.ts(K3B, S3B, float(2.0 ** -16), None, ALU.mult, inc=ve)

        P.phase("l1f2")
        P.memset(MXP, 0.0, inc=gp)
        t_mx2 = (gp, gp.n)
        sp_free = [None, None]
        for st_i in range(NST):
            sl = st_i % 2
            gsl = XNKQ[:, sl * 1024:sl * 1024 + 512]
            t_g = P.dma("sync", gsl, gpre_sc[:, st_i * 512:(st_i + 1) * 512], ld,
                        wait=[sp_free[sl]] if sp_free[sl] else [(ve, ve.n), (ac, ac.n)])
            t_w = P.act(UA[:, 0:512], gsl, AF.Identity, scale=RS3B,
                        wait=[t_g, t_k3, (ve, ve.n)], inc=ac)
            P.ts(SQS[:, 0:512], UA[:, 0:512], C32, C32, ALU.add, ALU.subtract, wait=[t_w])
            P.ts(UA[:, 512:1024], UA[:, 0:512], C64, C64, ALU.add, ALU.subtract)
            t_qg = P.stt(gsl, SQS[:, 0:512], GAM, UA[:, 512:1024], ALU.mult, ALU.add,
                         inc=ve)
            P.act(UA[:, 0:512], gsl, AF.Sigmoid, scale=S3B, wait=[t_qg])
            t_w3 = P.act(UA[:, 512:1024], UA[:, 0:512], AF.Identity, scale=65536.0,
                         inc=ac)
            P.ts(SQS[:, 0:512], UA[:, 512:1024], C32, C32, ALU.add, ALU.subtract,
                 wait=[t_w3])
            P.ts(UA[:, 1024:1536], UA[:, 512:1024], C64, C64, ALU.add, ALU.subtract)
            P.stt(UA[:, 512:1024], SQS[:, 0:512], GAM, UA[:, 1024:1536],
                  ALU.mult, ALU.add)
            P.ts(UA[:, 512:1024], UA[:, 512:1024], 65535.0, None, ALU.min)
            t_m2 = P.tt(gsl, gsl, UA[:, 512:1024], ALU.mult, inc=ve)
            t_rd = P.red(SCRR, gsl, ALU.max, absval=True, wait=[t_m2], inc=ve)
            P.tt(MXP, MXP, SCRR, ALU.max, wait=[t_mx2], inc=ve)
            t_sp = P.dma("gpsimd", m2_sc[:, st_i * 512:(st_i + 1) * 512], gsl, st,
                         wait=[t_m2, t_rd])
            sp_free[sl] = t_sp
        t_p4 = finish_partial(0)
        t_m4 = global_max(1, [t_p4], None)
        t_mb4 = bcast_scalar(0, MBC, [t_m4])
        P.tt(S4B, MBC, K3B, ALU.mult, wait=[t_mb4])
        P.op("vector", lambda e: e.tensor_scalar(S4B, S4B, R327, 1e-12, ALU.mult, ALU.max))
        P.recip(RS4B, S4B)
        P.tt(C34B, K3B, RS4B, ALU.mult)
        t_c34 = P.tt(K45B, S4B, S2B, ALU.mult, inc=ve)

        P.phase("l1f3")
        P.memset(MXP, 0.0, inc=gp)
        t_mx3 = (gp, gp.n)
        sp_free = [None, None]
        for st_i in range(NST):
            sl = st_i % 2
            msl = XNKQ[:, sl * 1024:sl * 1024 + 512]
            usl = XNKQ[:, 2048 + sl * 1024:2048 + sl * 1024 + 512]
            t_m = P.dma("sync", msl, m2_sc[:, st_i * 512:(st_i + 1) * 512], ld,
                        wait=[sp_free[sl]] if sp_free[sl] else [(ve, ve.n), (ac, ac.n)])
            t_u = P.dma("sync", usl, upre_sc[:, st_i * 512:(st_i + 1) * 512], ld,
                        wait=[sp_free[sl]] if sp_free[sl] else [])
            t_w = P.act(UA[:, 0:512], msl, AF.Identity, scale=C34B,
                        wait=[t_m, t_c34, (ve, ve.n)], inc=ac)
            P.ts(SQS[:, 0:512], UA[:, 0:512], C32, C32, ALU.add, ALU.subtract, wait=[t_w])
            P.ts(UA[:, 512:1024], UA[:, 0:512], C64, C64, ALU.add, ALU.subtract)
            P.stt(msl, SQS[:, 0:512], GAM, UA[:, 512:1024], ALU.mult, ALU.add, inc=ve)
            t_w2 = P.act(UA[:, 0:512], usl, AF.Identity, scale=RS2B,
                         wait=[t_u, (ve, ve.n)], inc=ac)
            P.ts(SQS[:, 0:512], UA[:, 0:512], C32, C32, ALU.add, ALU.subtract, wait=[t_w2])
            P.ts(UA[:, 512:1024], UA[:, 0:512], C64, C64, ALU.add, ALU.subtract)
            P.stt(usl, SQS[:, 0:512], GAM, UA[:, 512:1024], ALU.mult, ALU.add)
            t_mo = P.tt(msl, msl, usl, ALU.mult, inc=ve)
            t_rd = P.red(SCRR, msl, ALU.max, absval=True, wait=[t_mo], inc=ve)
            P.tt(MXP, MXP, SCRR, ALU.max, wait=[t_mx3], inc=ve)
            t_sp = P.dma("gpsimd", gpre_sc[:, st_i * 512:(st_i + 1) * 512], msl, st,
                         wait=[t_mo, t_rd])
            sp_free[sl] = t_sp
        t_p5 = finish_partial(0)
        t_m5 = global_max(1, [t_p5], None)
        t_mb5 = bcast_scalar(0, MBC, [t_m5])
        P.tt(S5B, MBC, K45B, ALU.mult, wait=[t_mb5])
        P.op("vector", lambda e: e.tensor_scalar(S5B, S5B, R327, 1e-12, ALU.mult, ALU.max))
        P.recip(RS4B, S5B)
        t_c45 = P.tt(C45B, K45B, RS4B, ALU.mult, inc=ve)

        P.phase("l1f4")
        wd_prev = [t_wd, t_wd]
        q5_free = [None, None]
        for ch in range(NCH1):
            cols = slice(ch * CT1, (ch + 1) * CT1)
            t_last = None
            for fb in range(NFB1):
                sl = fb % 2
                msl = XNKQ[:, sl * 512:sl * 512 + CT1]
                t_m = P.dma("sync", msl, gpre_sc[:, fb * T + cols.start:fb * T + cols.stop],
                            ld, wait=[q5_free[sl]] if q5_free[sl] else
                            [(ve, ve.n), (ac, ac.n)])
                t_w = P.act(UA[:, 0:CT1], msl, AF.Identity, scale=C45B,
                            wait=[t_m, t_c45, (ve, ve.n)], inc=ac)
                P.ts(SQS[:, 0:CT1], UA[:, 0:CT1], C32, C32, ALU.add, ALU.subtract,
                     wait=[t_w])
                P.ts(UA[:, 512:512 + CT1], UA[:, 0:CT1], C64, C64, ALU.add, ALU.subtract)
                t_q5 = P.stt(msl, SQS[:, 0:CT1], GAM, UA[:, 512:512 + CT1],
                             ALU.mult, ALU.add, inc=ve)
                t_wd2 = P.dma("gpsimd", WS[:, 2048 + sl * 1024:2048 + (sl + 1) * 1024],
                              wdt_sc[:, fb * 1024:(fb + 1) * 1024], ld2,
                              wait=[wd_prev[sl]] if wd_prev[sl] else [])
                for oc in range(8):
                    ob = BK[oc]
                    w0 = [t_q5, t_wd2]
                    if fb == 0:
                        w0 = w0 + ob.wwait()
                    t_last = P.mm(ob.pb[0:128, 0:CT1],
                                  WS[:, 2048 + sl * 1024 + oc * 128:2048 + sl * 1024 + (oc + 1) * 128],
                                  msl, start=(fb == 0), stop=(fb == NFB1 - 1),
                                  wait=w0, inc=te if oc == 7 else None)
                    if fb == NFB1 - 1:
                        ob.wrote(t_last)
                q5_free[sl] = t_last
                wd_prev[sl] = t_last
            for oc in range(8):
                ob = BK[oc]
                t_ra = P.stt(RES[:, oc * T + cols.start:oc * T + cols.stop],
                             ob.pb[0:128, 0:CT1], S5B,
                             RES[:, oc * T + cols.start:oc * T + cols.stop],
                             ALU.mult, ALU.add, wait=[t_last], inc=ve)
                ob.read(t_ra)
        resid_ready = (ve, ve.n)

    P.phase("out")
    res_done = [(ve, ve.n), (te, te.n), (ac, ac.n)]
    t_out = None
    for c in range(8):
        t_out = P.dma("gpsimd", y_h[c * 128:(c + 1) * 128, :], RES[:, c * T:(c + 1) * T],
                      st, wait=res_done)
    dump_req = [d[5:].split(",") for d in dbg if d.startswith("dump:")]
    if dump_req:
        avail = {"XNKQ": XNKQ[:, :], "UA": UA[:, :], "KV0": KV0[:, :],
                 "STA": STA[:, :], "EQF": EQF[:, :], "WS": WS[:, :],
                 "SQS": SQS[:, :], "CONST": CONST[:, :], "STB": STB[:, :],
                 "MKA": MKA, "DKT": DKT, "NDKT": NDKT, "MKROW": MKROW,
                 "MH16": MH16, "EHB": EHB, "EWTM": EWTM, "BTT": BTT,
                 "RDEN": RDEN, "DQF": DQF, "LNR": LNR[:, :]}
        for nm in dump_req[0]:
            src = avail[nm]
            dt_ = nc.dram_tensor("dump_" + nm, list(src.shape), F32,
                                 kind="ExternalOutput").ap()
            t_out = P.dma("gpsimd", dt_[:, :], src, st, wait=res_done)
    P.op("gpsimd", lambda e: e.engine_nop(), wait=[t_out])

    P.emit()
    return nc


_CACHE = {}


def _consts():
    H = 16
    ident = np.eye(128, dtype=np.float32)
    hones = np.zeros((128, 8 * H), dtype=np.float32)
    for c in range(8):
        for p in range(128):
            hones[p, c * 16 + 2 * c + (p >= 64)] = 0.5
    indb = np.zeros((16, 8 * 128), dtype=np.float32)
    for c in range(8):
        for p in range(128):
            indb[2 * c + (p >= 64), c * 128 + p] = 1.0
    indh = np.zeros((16, 16 * 64), dtype=np.float32)
    for h in range(16):
        indh[h, h * 64:(h + 1) * 64] = 1.0
    return ident, hones, indb, indh


def make_in_maps(inputs, n_cores=8):
    x = np.asarray(inputs["L0_attn_x"], dtype=np.float32)
    B, N, D = x.shape
    T = (B * N) // n_cores
    ident, hones, indb, indh = _consts()
    common = {
        "l0_wqkv": np.ascontiguousarray(inputs["L0_attn_W_qkv"], dtype=np.float32),
        "l0_wout": np.ascontiguousarray(inputs["L0_attn_W_out"], dtype=np.float32),
        "l0_proj": np.ascontiguousarray(inputs["L0_attn_proj"], dtype=np.float32),
        "l0_w1": np.ascontiguousarray(inputs["L0_ffn_W1"], dtype=np.float32),
        "l0_b1": np.ascontiguousarray(inputs["L0_ffn_b1"], dtype=np.float32),
        "l0_w2": np.ascontiguousarray(inputs["L0_ffn_W2"], dtype=np.float32),
        "l0_b2": np.ascontiguousarray(inputs["L0_ffn_b2"], dtype=np.float32),
        "l1_qkvw": np.ascontiguousarray(inputs["L1_attn_qkv_w"], dtype=np.float32),
        "l1_qkvb": np.ascontiguousarray(inputs["L1_attn_qkv_b"], dtype=np.float32),
        "l1_projw": np.ascontiguousarray(inputs["L1_attn_proj_w"], dtype=np.float32),
        "l1_projb": np.ascontiguousarray(inputs["L1_attn_proj_b"], dtype=np.float32),
        "l1_pm": np.ascontiguousarray(inputs["L1_attn_projection_matrix"], dtype=np.float32),
        "l1_wg": np.ascontiguousarray(inputs["L1_ffn_w_gate"], dtype=np.float32),
        "l1_wu": np.ascontiguousarray(inputs["L1_ffn_w_up"], dtype=np.float32),
        "l1_wd": np.ascontiguousarray(inputs["L1_ffn_w_down"], dtype=np.float32),
        "ident": ident, "hones": hones, "indb": indb, "indh": indh,
    }
    halves = n_cores // B
    in_maps = []
    for c in range(n_cores):
        b, hf = c // halves, c % halves
        xc = np.ascontiguousarray(x[b, hf * T:(hf + 1) * T, :].T)
        m = dict(common)
        m["x"] = xc
        in_maps.append(m)
    return in_maps, T, B, N, D


def kernel(**inputs):
    in_maps, T, B, N, D = make_in_maps(inputs)
    key = (T,)
    if key not in _CACHE:
        _CACHE[key] = build_nc(T=T)
    nc = _CACHE[key]
    res = run_bass_kernel_spmd(nc, in_maps, core_ids=list(range(8)))
    out = np.empty((B, N, D), dtype=np.float32)
    for c in range(8):
        b, hf = c // 2, c % 2
        out[b, hf * T:(hf + 1) * T, :] = res.results[c]["y"].T
    return out


# version 07 (final)
# speedup vs baseline: 1.0518x; 1.0518x over previous
import numpy as np
import concourse.bass as bass
import concourse.mybir as mybir
from concourse.bass_utils import run_bass_kernel_spmd

F32 = mybir.dt.float32
AF = mybir.ActivationFunctionType
ALU = mybir.AluOpType
AX = mybir.AxisListType

C32 = float(3 * 2**22)
C64 = float(3 * 2**51)
EPS0 = 1e-6
LNEPS = 1e-5
NEG = -3.0e38

ENGINES = ("sync", "tensor", "vector", "scalar", "gpsimd")


class S:
    def __init__(self, nc, name):
        self.h = nc.alloc_semaphore(name)
        self.n = 0
        self.name = name

    def bump(self, k=1):
        self.n += k
        return self.n


class Prog:

    def __init__(self, nc):
        self.nc = nc
        self.phases = []
        self.ph = None
        self._vv = {"vector": S(nc, "vvv"), "scalar": S(nc, "vvs")}
        self._vhist = {"vector": [], "scalar": []}
        self.n_edges = 0

    def phase(self, name):
        self.ph = {"name": name, "ops": {e: [] for e in ENGINES},
                   "lastwait": {e: {} for e in ENGINES}}
        self.phases.append(self.ph)

    @staticmethod
    def _box(a):
        try:
            nm = a.tensor.name
            ap = a.ap
            pitch, np_ = ap[0]
            off = a.offset
            p0, f0 = off // pitch, off % pitch
            f1 = f0 + 1
            for s, n in ap[1:]:
                if s >= pitch:
                    return None
                f1 += (n - 1) * s
            if f1 > pitch:
                return None
            return (nm, p0, p0 + np_, f0, f1)
        except Exception:
            return None

    @staticmethod
    def _ovl(w, r):
        if w is None or r is None:
            return True
        return (w[0] == r[0] and w[1] < r[2] and r[1] < w[2]
                and w[3] < r[4] and r[3] < w[4])

    def op(self, eng, fn, wait=(), inc=None, inc_n=1, reads=None, writes=None):
        rec = {"extra": False}
        hist = self._vhist.get(eng)
        if hist is not None:
            vv = self._vv[eng]
            rb = [None] if reads is None else [self._box(a) for a in reads
                                               if a is not None]
            wb = [None] if writes is None else [self._box(a) for a in writes
                                                if a is not None]
            need = None
            for p in reversed(hist[-16:]):
                if (any(self._ovl(w, r) for w in p["w"] for r in rb)
                        or any(self._ovl(w, w2) for w in p["w"] for w2 in wb)
                        or any(self._ovl(r0, w2) for r0 in p["r"] for w2 in wb)):
                    need = p
                    break
            if need is not None:
                if need["tok"] is None:
                    need["rec"]["extra"] = True
                    need["tok"] = (vv, vv.bump())
                wait = list(wait) + [need["tok"]]
                self.n_edges += 1
        lw = self.ph["lastwait"][eng]
        wl = []
        for w in wait:
            if w is None:
                continue
            (s, v) = w
            if lw.get(s.name, -1) < v:
                wl.append((s.h, v))
                lw[s.name] = v
        tok = None
        if inc is not None:
            tok = (inc, inc.bump(inc_n))
        vvh = self._vv[eng].h if hist is not None else None

        def f(e):
            for sh, v in wl:
                e.wait_ge(sh, v)
            r = fn(e)
            if inc is not None:
                r.then_inc(inc.h, inc_n)
            if rec["extra"]:
                r.then_inc(vvh, 1)
        self.ph["ops"][eng].append(f)
        if hist is not None:
            hist.append({"w": wb, "r": rb, "tok": tok, "rec": rec})
            if len(hist) > 48:
                del hist[:16]
        return tok

    def dma(self, q, out, in_, sem, wait=(), slow=False):
        if slow:
            return self.op(q, lambda e: e.dma_start(out=out, in_=in_,
                                                    allow_slow_non_contiguous=True),
                           wait, sem, 16)
        return self.op(q, lambda e: e.dma_start(out=out, in_=in_), wait, sem, 16)

    def mm(self, out, lhsT, rhs, start=True, stop=True, wait=(), inc=None):
        return self.op("tensor",
                       lambda e: e.matmul(out, lhsT, rhs, start=start, stop=stop),
                       wait, inc)

    def tp(self, out, in_, ident, np_, wait=(), inc=None):
        return self.op("tensor",
                       lambda e: e.matmul(out, in_, ident[0:np_, 0:np_],
                                          start=True, stop=True, is_transpose=True),
                       wait, inc)

    def act(self, out, in_, func, bias=None, scale=None, accum_out=None, wait=(), inc=None):
        kw = {}
        rd = [in_]
        if bias is not None:
            kw["bias"] = bias
            if hasattr(bias, "offset"):
                rd.append(bias)
        if scale is not None:
            kw["scale"] = scale
            if hasattr(scale, "offset"):
                rd.append(scale)
        wr = [out]
        if accum_out is not None:
            kw["accum_out"] = accum_out
            wr.append(accum_out)
        return self.op("scalar", lambda e: e.activation(out, in_, func, **kw), wait, inc,
                       reads=rd, writes=wr)

    def cp(self, eng, out, in_, wait=(), inc=None):
        return self.op(eng, lambda e: e.tensor_copy(out, in_), wait, inc,
                       reads=[in_], writes=[out])

    def ts(self, out, in_, s1, s2, op0, op1=None, wait=(), inc=None, eng="vector"):
        if op1 is None:
            return self.op(eng, lambda e: e.tensor_scalar(out, in_, s1, None, op0),
                           wait, inc, reads=[in_], writes=[out])
        return self.op(eng, lambda e: e.tensor_scalar(out, in_, s1, s2, op0, op1),
                       wait, inc, reads=[in_], writes=[out])

    def tt(self, out, a, b, alu, wait=(), inc=None, eng="vector"):
        return self.op(eng, lambda e: e.tensor_tensor(out, a, b, alu), wait, inc,
                       reads=[a, b], writes=[out])

    def stt(self, out, in0, scal, in1, op0, op1, wait=(), inc=None, eng="vector"):
        return self.op(eng, lambda e: e.scalar_tensor_tensor(out, in0, scal, in1, op0, op1),
                       wait, inc, reads=[in0, scal, in1], writes=[out])

    def red(self, out, in_, alu, absval=None, negate=None, wait=(), inc=None):
        return self.op("vector",
                       lambda e: e.tensor_reduce(out, in_, AX.X, alu,
                                                 apply_absolute_value=absval, negate=negate),
                       wait, inc, reads=[in_], writes=[out])

    def recip(self, out, in_, wait=(), inc=None):
        return self.op("vector", lambda e: e.reciprocal(out, in_), wait, inc,
                       reads=[in_], writes=[out])

    def memset(self, ap, val, wait=(), inc=None, eng="gpsimd"):
        return self.op(eng, lambda e: e.memset(ap, val), wait, inc,
                       reads=[], writes=[ap])

    def coll(self, kind, alu, groups, ins, outs, wait=(), inc=None):
        return self.op("gpsimd",
                       lambda e: e.collective_compute(kind, alu, replica_groups=groups,
                                                      ins=ins, outs=outs),
                       wait, inc)

    def emit(self):
        nc = self.nc
        for ph in self.phases:
            with nc.Block(ph["name"]) as block:
                for ename in ENGINES:
                    ops = ph["ops"][ename]
                    if not ops:
                        continue

                    def body(eng, ops=ops):
                        for f in ops:
                            f(eng)
                    getattr(block, ename)(body)


class Bank:

    def __init__(self, pb):
        self.pb = pb
        self.readers = []
        self.wtok = None

    def wwait(self):
        w, self.readers = self.readers, []
        return w

    def wrote(self, tok):
        self.wtok = tok

    def rwait(self):
        return [self.wtok] if self.wtok else []

    def read(self, tok):
        self.readers.append(tok)


def build_nc(T=4096, CTA=256, blocks=("l0a", "l0f", "l1a", "l1f"),
             detect_races=False, n_cores=8, dbg=()):
    nc = bass.Bass("TRN2", target_bir_lowering=False,
                   detect_race_conditions=detect_races)
    D, H, HD = 1024, 16, 64
    NCH_A = T // CTA
    NTIL_A = CTA // 128

    x_h = nc.dram_tensor("x", [D, T], F32, kind="ExternalInput").ap()
    y_h = nc.dram_tensor("y", [D, T], F32, kind="ExternalOutput").ap()
    wqkv0 = nc.dram_tensor("l0_wqkv", [D, 3 * D], F32, kind="ExternalInput").ap()
    wout0 = nc.dram_tensor("l0_wout", [D, D], F32, kind="ExternalInput").ap()
    proj0 = nc.dram_tensor("l0_proj", [H, 128, HD], F32, kind="ExternalInput").ap()
    ident_h = nc.dram_tensor("ident", [128, 128], F32, kind="ExternalInput").ap()
    hones_h = nc.dram_tensor("hones", [128, 8 * H], F32, kind="ExternalInput").ap()
    indb_h = nc.dram_tensor("indb", [16, 8 * 128], F32, kind="ExternalInput").ap()
    indh_h = nc.dram_tensor("indh", [16, 16 * 64], F32, kind="ExternalInput").ap()
    w1 = nc.dram_tensor("l0_w1", [4096, D], F32, kind="ExternalInput").ap()
    b1 = nc.dram_tensor("l0_b1", [4096], F32, kind="ExternalInput").ap()
    w2 = nc.dram_tensor("l0_w2", [D, 4096], F32, kind="ExternalInput").ap()
    b2 = nc.dram_tensor("l0_b2", [D], F32, kind="ExternalInput").ap()
    qkvw1 = nc.dram_tensor("l1_qkvw", [3 * D, D], F32, kind="ExternalInput").ap()
    qkvb1 = nc.dram_tensor("l1_qkvb", [3 * D], F32, kind="ExternalInput").ap()
    projw1 = nc.dram_tensor("l1_projw", [D, D], F32, kind="ExternalInput").ap()
    projb1 = nc.dram_tensor("l1_projb", [D], F32, kind="ExternalInput").ap()
    pm1 = nc.dram_tensor("l1_pm", [64, HD], F32, kind="ExternalInput").ap()
    wg1 = nc.dram_tensor("l1_wg", [3072, D], F32, kind="ExternalInput").ap()
    wu1 = nc.dram_tensor("l1_wu", [3072, D], F32, kind="ExternalInput").ap()
    wd1 = nc.dram_tensor("l1_wd", [D, 3072], F32, kind="ExternalInput").ap()

    kv0_bi = nc.dram_tensor("kv0_bi", [128, 2080], F32)
    kv0_bo = nc.dram_tensor("kv0_bo", [128, 2080], F32)
    mh0_bi = nc.dram_tensor("mh0_bi", [16, 1], F32)
    mh0_bo = nc.dram_tensor("mh0_bo", [16, 1], F32)
    wq1t_sc = nc.dram_tensor("wq1t_sc", [128, 32 * 1024], F32)
    wq2t_sc = nc.dram_tensor("wq2t_sc", [128, 32 * 1024], F32)
    qkv1t_sc = nc.dram_tensor("qkv1t_sc", [128, 8 * 3072], F32)
    proj1t_sc = nc.dram_tensor("proj1t_sc", [128, 8 * 1024], F32)
    kv1_bi = nc.dram_tensor("kv1_bi", [64, 1040], F32)
    kv1_bo = nc.dram_tensor("kv1_bo", [64, 1040], F32)
    row_bi = nc.dram_tensor("row_bi", [1, 2048], F32)
    row_bo = nc.dram_tensor("row_bo", [1, 2048], F32)
    wut_sc = nc.dram_tensor("wut_sc", [128, 24 * 1024], F32)
    wgt_sc = nc.dram_tensor("wgt_sc", [128, 24 * 1024], F32)
    wdt_sc = nc.dram_tensor("wdt_sc", [128, 24 * 1024], F32)
    upre_sc = nc.dram_tensor("upre_sc", [128, 24 * T], F32)
    gpre_sc = nc.dram_tensor("gpre_sc", [128, 24 * T], F32)
    m2_sc = nc.dram_tensor("m2_sc", [128, 24 * T], F32)
    mx_bi = nc.dram_tensor("mx_bi", [1, 4], F32)
    mx_bo = nc.dram_tensor("mx_bo", [1, 4], F32)

    sb = nc.alloc_sbuf_tensor
    RES = sb("RES", [128, 8 * T], F32)
    XNKQ = sb("XNKQ", [128, 4096], F32)
    XN = XNKQ[:, 0:8 * CTA]
    KQ = XNKQ[:, 8 * CTA:16 * CTA]
    UA = sb("UA", [128, 2048], F32)
    EQF = sb("EQF", [128, 2 * CTA], F32)
    EKT = EQF[:, 0:2 * CTA]
    KV0 = sb("KV0", [128, 2080], F32)
    STA = sb("STA", [128, 672], F32)
    STB = sb("STB", [16, 2 * CTA], F32)
    WS = sb("WS", [128, 4096], F32)
    SQS = sb("SQS", [128, 2 * CTA], F32)
    LNR = sb("LNR", [1, 2 * CTA], F32)
    CONST = sb("CONST", [128, 576], F32)
    PMT0 = sb("PMT0", [128, 8 * 128], F32)
    INDB = sb("INDB", [16, 8 * 128], F32)
    INDH = sb("INDH", [16, 16 * 64], F32)

    VT = UA[:, 0:NTIL_A * 1024]
    NUMF = UA[:, 0:8 * CTA]

    IDENT = CONST[:, 0:128]
    HONES = CONST[:, 128:256]
    ONES = CONST[:, 256:384]
    GAM = CONST[:, 384:385]
    ONE1 = CONST[:, 385:386]

    B1S = CONST[:, 400:432]
    B2S = CONST[:, 432:440]
    RG1 = CONST[:, 440:441]
    RG2 = CONST[:, 441:442]
    QBS = CONST[:, 448:472]
    PJBS = CONST[:, 472:480]
    PMT1 = CONST[:, 480:544]
    GAB = CONST[:, 544:576]
    MKA = STA[:, 0:16]
    DKT = STA[:, 16:32]
    NDKT = STA[:, 32:48]
    MQT = STA[:, 48:64]
    WNT = STA[:, 64:80]
    MH16 = STA[0:16, 88:89]
    EH16 = STA[0:16, 89:90]
    SCR1 = STA[:, 96:112]
    SCRR = STA[:, 112:113]
    MKROW = STA[0:16, 128:256]

    RDEN = STB[0:16, 0:CTA]
    DQF = STB[0:16, CTA:2 * CTA]
    RMXR = STB[0:16, CTA:2 * CTA]
    KV1 = KV0[0:64, 0:1040]
    KSR = KV0[64:65, 0:1024]
    QBV = KV0[64:65, 1040:2064]
    VSFR = KV0[65:66, 1040:2064]
    ONE65 = KV0[65:66, 0:1024]
    VSUM = KV0[96:97, 0:1024]
    BSTG = STA[0:1, 384:384 + CTA]
    ESB2 = STA[:, 640:656]
    XSUM = STA[:, 344:352]
    ESB = STA[:, 256:272]
    ETB = STA[:, 272:288]
    EWTM = STA[:, 256:288]
    BTT = STA[:, 288:320]
    DENT = STA[:, 320:336]
    EHB = STA[:, 336:352]
    EHROW = STA[0:1, 352:368]

    pt = nc.alloc_psum_tensor
    PB = [pt(f"PB{i}", [128, 512], F32) for i in range(8)]
    BK = [Bank(PB[i]) for i in range(8)]

    P = Prog(nc)
    ld = S(nc, "ld")
    ld2 = S(nc, "ld2")
    st = S(nc, "st")
    te = S(nc, "te")
    ve = S(nc, "ve")
    ac = S(nc, "ac")
    gp = S(nc, "gp")
    cc = S(nc, "cc")

    P.phase("init")
    x_toks = []
    for c in range(8):
        x_toks.append(P.dma("sync", RES[:, c * T:(c + 1) * T],
                            x_h[c * 128:(c + 1) * 128, :], ld))
    t_id = P.dma("sync", IDENT, ident_h[:, :], ld)
    P.dma("sync", HONES, hones_h[:, :], ld)
    t_hon = (ld, ld.n)
    P.dma("sync", INDB[0:16, :], indb_h[:, :], ld)
    t_indb = (ld, ld.n)
    P.memset(ONES, 1.0, inc=gp)
    P.memset(ONE1, 1.0, inc=gp)
    t_ones = (gp, gp.n)
    P.ts(GAM, ONE1, C64, C64, ALU.add, ALU.subtract, wait=[t_ones])
    t_gam = P.tt(GAM, ONE1, GAM, ALU.subtract, inc=ve)
    pm_stage = WS[:, 0:64]
    pm_tmp = WS[:, 64:192]
    t_pmt = None
    for h in range(H):
        t_pm = P.dma("sync", pm_stage, proj0[h, :, :], ld,
                     wait=[t_pmt] if t_pmt else [])
        t_tp = P.tp(PB[0][0:64, 0:128], pm_stage[0:128, 0:64], IDENT, 128,
                    wait=[t_pm, t_id] + BK[0].wwait(), inc=te)
        BK[0].wrote(t_tp)
        hp, r0 = h // 2, 64 * (h % 2)
        if r0 == 0:
            t_pmt = P.act(PMT0[0:64, hp * 128:(hp + 1) * 128], PB[0][0:64, 0:128],
                          AF.Identity, scale=0.125, wait=[t_tp], inc=ac)
            BK[0].read(t_pmt)
        else:
            t_ev = P.act(pm_tmp[0:64, 0:128], PB[0][0:64, 0:128],
                         AF.Identity, scale=0.125, wait=[t_tp], inc=ac)
            BK[0].read(t_ev)
            t_pmt = P.dma("gpsimd", PMT0[64:128, hp * 128:(hp + 1) * 128],
                          pm_tmp[0:64, 0:128], st, wait=[t_ev])

    def ln_chunk(cols, xn_out, bs, bb, resid_ready):
        ct = cols.stop - cols.start
        t_sum = None
        for dc in range(8):
            rchunk = RES[:, dc * T + cols.start: dc * T + cols.stop]
            t_sq = P.act(SQS[:, (dc % 2) * ct:(dc % 2 + 1) * ct], rchunk, AF.Square,
                         wait=[resid_ready, (te, te.n)], inc=ac)
            w0 = (BK[bs].wwait() if dc == 0 else []) + [t_sq]
            P.mm(BK[bs].pb[0:1, 0:ct], ONES[:, 0:1], rchunk,
                 start=(dc == 0), stop=False, wait=w0, inc=te)
            t_sum = P.mm(BK[bs].pb[0:1, ct:2 * ct], ONES[:, 0:1],
                         SQS[:, (dc % 2) * ct:(dc % 2 + 1) * ct],
                         start=False, stop=(dc == 7), inc=te)
        BK[bs].wrote(t_sum)
        P.ts(LNR[0:1, 0:ct], BK[bs].pb[0:1, 0:ct], 1.0 / 1024.0, None, ALU.mult,
             wait=[t_sum])
        t_st2 = P.ts(LNR[0:1, ct:2 * ct], BK[bs].pb[0:1, ct:2 * ct], 1.0 / 1024.0, None,
                     ALU.mult, inc=ve)
        BK[bs].read(t_st2)
        P.tt(SQS[0:1, 0:ct], LNR[0:1, 0:ct], LNR[0:1, 0:ct], ALU.mult)
        P.tt(SQS[0:1, 0:ct], LNR[0:1, ct:2 * ct], SQS[0:1, 0:ct], ALU.subtract)
        t_var = P.ts(SQS[0:1, 0:ct], SQS[0:1, 0:ct], LNEPS, None, ALU.add, inc=ve)
        t_sqr = P.act(SQS[0:1, ct:2 * ct], SQS[0:1, 0:ct], AF.Sqrt, wait=[t_var], inc=ac)
        P.recip(LNR[0:1, ct:2 * ct], SQS[0:1, ct:2 * ct], wait=[t_sqr])
        P.tt(LNR[0:1, 0:ct], LNR[0:1, 0:ct], LNR[0:1, ct:2 * ct], ALU.mult)
        t_ab = P.ts(LNR[0:1, 0:ct], LNR[0:1, 0:ct], -1.0, None, ALU.mult, inc=ve)
        P.mm(BK[bb].pb[0:128, 0:ct], ONES[0:1, 0:128], LNR[0:1, ct:2 * ct],
             wait=[t_ab] + BK[bb].wwait())
        t_bc = P.mm(BK[bb].pb[0:128, ct:2 * ct], ONES[0:1, 0:128], LNR[0:1, 0:ct], inc=te)
        BK[bb].wrote(t_bc)
        t_xn = None
        for dc in range(8):
            rchunk = RES[:, dc * T + cols.start: dc * T + cols.stop]
            P.tt(xn_out[:, dc * ct:(dc + 1) * ct], rchunk, BK[bb].pb[0:128, 0:ct],
                 ALU.mult, wait=[t_bc])
            t_xn = P.tt(xn_out[:, dc * ct:(dc + 1) * ct], xn_out[:, dc * ct:(dc + 1) * ct],
                        BK[bb].pb[0:128, ct:2 * ct], ALU.add, inc=ve)
        BK[bb].read(t_xn)
        return t_xn

    def dstat(src, cols_ct, bs, src_ready):
        t_d = None
        for dc in range(8):
            t_sq = P.act(SQS[:, (dc % 2) * cols_ct:(dc % 2 + 1) * cols_ct],
                         src[:, dc * cols_ct:(dc + 1) * cols_ct], AF.Square,
                         wait=[src_ready, (te, te.n)], inc=ac)
            w0 = (BK[bs].wwait() if dc == 0 else []) + [t_sq, t_hon]
            t_d = P.mm(BK[bs].pb[0:16, 0:cols_ct], HONES[:, dc * 16:(dc + 1) * 16],
                       SQS[:, (dc % 2) * cols_ct:(dc % 2 + 1) * cols_ct],
                       start=(dc == 0), stop=(dc == 7), wait=w0, inc=te)
        BK[bs].wrote(t_d)
        t_dc = P.cp("vector", DQF, BK[bs].pb[0:16, 0:cols_ct], wait=[t_d], inc=ve)
        BK[bs].read(t_dc)
        return t_dc

    resid_ready = x_toks[-1]

    if "l0a" in blocks:
        P.phase("l0a")
        P.memset(MKA, NEG, inc=gp)
        P.memset(bass.AP(tensor=KV0, offset=64, ap=[[2080, 128], [65, 32]]), 0.0, inc=gp)
        t_init = (gp, gp.n)
        wk_slot = [WS[:, 0:1024], WS[:, 1024:2048]]
        wv_slot = WS[:, 2048:4096]
        ks_last = [None, None]
        ekt_free = [None, None]
        kv_open = [False] * 4
        t_kv_last = [None] * 4
        for ch in range(NCH_A):
            cols = slice(ch * CTA, (ch + 1) * CTA)
            t_xn = ln_chunk(cols, XN, 5, 5, resid_ready)
            for kc in range(8):
                sl = kc % 2
                w_ap = bass.AP(tensor=wqkv0.tensor, offset=(D + kc * 128),
                               ap=[[3 * D, 128], [128 * 3 * D, 8], [1, 128]])
                t_wk = P.dma("sync",
                             wk_slot[sl].rearrange("p (c j) -> p c j", j=128),
                             w_ap, ld, wait=[ks_last[sl]] if ks_last[sl] else [])
                bk = BK[0]
                for dc in range(8):
                    w0 = ([t_wk, t_xn] + bk.wwait()) if dc == 0 else []
                    t_kmm = P.mm(bk.pb[0:128, 0:CTA],
                                 wk_slot[sl][:, dc * 128:(dc + 1) * 128],
                                 XN[:, dc * CTA:(dc + 1) * CTA],
                                 start=(dc == 0), stop=(dc == 7), wait=w0,
                                 inc=te if dc == 7 else None)
                bk.wrote(t_kmm)
                t_kev = P.cp("vector", KQ[:, kc * CTA:(kc + 1) * CTA],
                             bk.pb[0:128, 0:CTA], wait=[t_kmm], inc=ve)
                bk.read(t_kev)
                ks_last[sl] = t_kev
            t_k = (ve, ve.n)
            if "nov" not in dbg:
                for vg in range(4):
                    w_ap = bass.AP(tensor=wqkv0.tensor, offset=(2 * D + vg * 256),
                                   ap=[[3 * D, 128], [128 * 3 * D, 8], [1, 256]])
                    t_wv = P.dma("sync",
                                 wv_slot.rearrange("p (c j) -> p c j", j=256),
                                 w_ap, ld, wait=[(te, te.n)])
                    for tl in range(NTIL_A):
                        bk = BK[0]
                        for dc in range(8):
                            w0 = ([t_wv, t_xn] + bk.wwait()) if dc == 0 else []
                            t_vmm = P.mm(bk.pb[0:128, 0:256],
                                         XN[:, dc * CTA + tl * 128: dc * CTA + (tl + 1) * 128],
                                         wv_slot[:, dc * 256:(dc + 1) * 256],
                                         start=(dc == 0), stop=(dc == 7), wait=w0,
                                         inc=te if dc == 7 else None)
                        bk.wrote(t_vmm)
                        t_vev = P.act(VT[:, tl * 1024 + vg * 256: tl * 1024 + (vg + 1) * 256],
                                      bk.pb[0:128, 0:256], AF.Identity, wait=[t_vmm], inc=ac)
                        bk.read(t_vev)
            t_v = (ac, ac.n)
            if "nofeat" in dbg:
                continue
            t_dkc = dstat(KQ, CTA, 5, t_k)
            for tl in range(NTIL_A):
                t_tpd = P.tp(BK[5].pb[0:128, 0:16], DQF[0:16, tl * 128:(tl + 1) * 128],
                             IDENT, 16, wait=[t_dkc] + BK[5].wwait(), inc=te)
                BK[5].wrote(t_tpd)
                t_dkt = P.cp("vector", DKT, BK[5].pb[0:128, 0:16], wait=[t_tpd], inc=ve)
                BK[5].read(t_dkt)
                t_ndk = P.ts(NDKT, DKT, -1.0, None, ALU.mult, inc=ve)
                for h in range(H if "nou" not in dbg else 0):
                    hp, r0 = h // 2, 64 * (h % 2)
                    ub = 2 + (h % 2)
                    t_u = P.mm(BK[ub].pb[0:128, 0:128],
                               KQ[r0:r0 + 64, hp * CTA + tl * 128: hp * CTA + (tl + 1) * 128],
                               PMT0[r0:r0 + 64, hp * 128:(hp + 1) * 128],
                               wait=[t_ndk, t_v, t_init] + BK[ub].wwait(), inc=te)
                    BK[ub].wrote(t_u)
                    P.red(SCRR, BK[ub].pb[0:128, 0:128], ALU.max, wait=[t_u])
                    t_mk = P.tt(MKA[:, h:h + 1], MKA[:, h:h + 1], SCRR, ALU.max, inc=ve)
                    BK[ub].read(t_mk)
                    es = h % 2
                    if "noek" in dbg:
                        continue
                    P.act(EKT[:, es * 256:es * 256 + 128], BK[ub].pb[0:128, 0:128],
                          AF.Exp, bias=NDKT[:, h:h + 1], scale=1.0,
                          wait=[t_u, t_mk] + ([ekt_free[es]] if ekt_free[es] else []))
                    t_ek = P.act(EKT[:, es * 256 + 128:es * 256 + 256],
                                 BK[ub].pb[0:128, 0:128], AF.Exp,
                                 bias=NDKT[:, h:h + 1], scale=-1.0, inc=ac)
                    BK[ub].read(t_ek)
                    for fc in range(2):
                        if "nokv" in dbg:
                            continue
                        kb = (1, 4, 6, 7)[2 * fc + h // 8]
                        kbi = (1, 4, 6, 7).index(kb)
                        stt0 = not kv_open[kbi]
                        kv_open[kbi] = True
                        is_last = (ch == NCH_A - 1 and tl == NTIL_A - 1
                                   and (h % 8) == 7)
                        t_kv = P.mm(PB[kb][0:128, (h % 8) * 64:(h % 8) * 64 + 64],
                                    EKT[:, es * 256 + fc * 128:es * 256 + (fc + 1) * 128],
                                    VT[:, tl * 1024 + h * 64: tl * 1024 + (h + 1) * 64],
                                    start=stt0, stop=is_last, wait=[t_ek], inc=te)
                        t_kv_last[kbi] = t_kv
                        ekt_free[es] = t_kv
                        t_ks = P.mm(BK[ub].pb[0:128, 256 + fc * 64 + (h // 2) * 2:
                                              257 + fc * 64 + (h // 2) * 2],
                                    EKT[:, es * 256 + fc * 128:es * 256 + (fc + 1) * 128],
                                    ONES[:, 0:1], start=True, stop=True,
                                    wait=[t_ek], inc=te)
                        BK[ub].wrote(t_ks)
                        kc0 = 65 * (fc * 16 + h) + 64
                        t_ka = P.tt(KV0[:, kc0:kc0 + 1], KV0[:, kc0:kc0 + 1],
                                    BK[ub].pb[0:128, 256 + fc * 64 + (h // 2) * 2:
                                              257 + fc * 64 + (h // 2) * 2],
                                    ALU.add, wait=[t_ks], inc=ve)
                        BK[ub].read(t_ka)
        if "notail" not in dbg:
            t_kvev = None
            if "noevac" not in dbg:
                for i, kb in enumerate((1, 4, 6, 7)):
                    out_ap = bass.AP(tensor=KV0, offset=65 * (8 * i),
                                     ap=[[2080, 128], [65, 8], [1, 64]])
                    in_ap = PB[kb].ap().rearrange("p (b j) -> p b j", j=64)
                    t_kvev = P.cp("vector", out_ap, in_ap, wait=[t_kv_last[i]], inc=ve)
                    BK[kb].read(t_kvev)
                    BK[kb].wtok = None
            t_tpm = P.tp(BK[5].pb[0:16, 0:128], MKA, IDENT, 128,
                         wait=[(ve, ve.n)] + BK[5].wwait(), inc=te)
            BK[5].wrote(t_tpm)
            t_mkc = P.cp("vector", MKROW, BK[5].pb[0:16, 0:128], wait=[t_tpm], inc=ve)
            BK[5].read(t_mkc)
            t_mh = P.red(MH16, MKROW, ALU.max, wait=[t_mkc], inc=ve)
            if "nocoll" in dbg:
                t_kvr = (ve, ve.n)
                t_l2 = t_mh
            else:
                t_s1 = P.dma("gpsimd", kv0_bi[:, :], KV0[:, :], st,
                             wait=[t_kvev, (ve, ve.n)])
                t_s2 = P.dma("gpsimd", mh0_bi[:, :], MH16, st, wait=[t_mh])
                pairs_g = [[2 * i, 2 * i + 1] for i in range(n_cores // 2)]
                t_c1 = P.coll("AllReduce", ALU.add, pairs_g, [kv0_bi[:, :].opt()],
                              [kv0_bo[:, :].opt()], wait=[t_s1], inc=cc)
                t_c2 = P.coll("AllReduce", ALU.max, pairs_g, [mh0_bi[:, :].opt()],
                              [mh0_bo[:, :].opt()], wait=[t_s2], inc=cc)
                t_kvr = P.dma("gpsimd", KV0[:, :], kv0_bo[:, :], st, wait=[t_c1])
                t_l2 = P.dma("gpsimd", MH16, mh0_bo[:, :], st, wait=[t_c2])
            if "noeh" not in dbg:
                t_ehx = P.act(EH16, MH16, AF.Exp, wait=[t_l2], inc=ac)
                t_eh0 = P.ts(EH16, EH16, float(256.0 * EPS0), None, ALU.mult, wait=[t_ehx], inc=ve)
                t_tpeh = P.tp(BK[5].pb[0:1, 0:16], EH16, IDENT, 16,
                              wait=[t_eh0] + BK[5].wwait(), inc=te)
                BK[5].wrote(t_tpeh)
                t_ehr = P.cp("vector", EHROW, BK[5].pb[0:1, 0:16], wait=[t_tpeh], inc=ve)
                BK[5].read(t_ehr)
                t_bceh = P.mm(BK[5].pb[0:128, 16:32], ONES[0:1, 0:128], EHROW,
                              wait=[t_ehr] + BK[5].wwait(), inc=te)
                BK[5].wrote(t_bceh)
                t_eh = P.cp("vector", EHB, BK[5].pb[0:128, 16:32], wait=[t_bceh], inc=ve)
                BK[5].read(t_eh)


        if "notail" not in dbg and "nob" not in dbg:
            P.phase("l0b")
            wq_slot = [WS[:, 0:1024], WS[:, 1024:2048]]
            wo_slot = [WS[:, 2048:3072], WS[:, 3072:4096]]
            q_last = [None, None]
            wo_last = [None, None]
            eqf_free = None
            for ch in range(NCH_A):
                cols = slice(ch * CTA, (ch + 1) * CTA)
                t_xn = ln_chunk(cols, XN, 5, 5, resid_ready)
                for qc in range(8):
                    sl = qc % 2
                    w_ap = bass.AP(tensor=wqkv0.tensor, offset=qc * 128,
                                   ap=[[3 * D, 128], [128 * 3 * D, 8], [1, 128]])
                    t_wq = P.dma("sync",
                                 wq_slot[sl].rearrange("p (c j) -> p c j", j=128),
                                 w_ap, ld, wait=[q_last[sl]] if q_last[sl] else [])
                    bk = BK[sl]
                    for dc in range(8):
                        w0 = ([t_wq, t_xn] + bk.wwait()) if dc == 0 else []
                        t_qmm = P.mm(bk.pb[0:128, 0:CTA],
                                     wq_slot[sl][:, dc * 128:(dc + 1) * 128],
                                     XN[:, dc * CTA:(dc + 1) * CTA],
                                     start=(dc == 0), stop=(dc == 7), wait=w0,
                                     inc=te if dc == 7 else None)
                    bk.wrote(t_qmm)
                    t_qev = P.cp("vector", KQ[:, qc * CTA:(qc + 1) * CTA],
                                 bk.pb[0:128, 0:CTA], wait=[t_qmm], inc=ve)
                    bk.read(t_qev)
                    q_last[sl] = t_qev
                t_q = (ve, ve.n)
                t_dqc = dstat(KQ, CTA, 5, t_q)
                for tl in range(NTIL_A):
                    t_tpd = P.tp(BK[5].pb[0:128, 0:16], DQF[0:16, tl * 128:(tl + 1) * 128],
                                 IDENT, 16, wait=[t_dqc] + BK[5].wwait(), inc=te)
                    BK[5].wrote(t_tpd)
                    t_dqt = P.cp("vector", DKT, BK[5].pb[0:128, 0:16], wait=[t_tpd], inc=ve)
                    BK[5].read(t_dqt)
                    for h in range(H):
                        hp, r0 = h // 2, 64 * (h % 2)
                        ub = 2 + (h % 2)
                        t_u = P.mm(BK[ub].pb[0:128, 0:128],
                                   KQ[r0:r0 + 64, hp * CTA + tl * 128: hp * CTA + (tl + 1) * 128],
                                   PMT0[r0:r0 + 64, hp * 128:(hp + 1) * 128],
                                   wait=BK[ub].wwait() + [t_dqt], inc=te)
                        BK[ub].wrote(t_u)
                        t_mq = P.red(MQT[:, h:h + 1], BK[ub].pb[0:128, 0:128], ALU.max,
                                     wait=[t_u], inc=ve)
                        BK[ub].read(t_mq)
                    P.tt(WNT, DKT, MQT, ALU.add)
                    t_wn = P.ts(WNT, WNT, 60.0, None, ALU.min, inc=ve)
                    t_ew = P.act(EWTM[:, 16 * tl:16 * tl + 16], WNT, AF.Exp,
                                 wait=[t_wn], inc=ac)
                t_ewr = (ac, ac.n)
                for h in range(H):
                    hp, r0 = h // 2, 64 * (h % 2)
                    ub = 2 + (h % 2)
                    t_uT = P.mm(BK[ub].pb[0:128, 0:CTA],
                                PMT0[r0:r0 + 64, hp * 128:(hp + 1) * 128],
                                KQ[r0:r0 + 64, hp * CTA: hp * CTA + CTA],
                                wait=BK[ub].wwait(), inc=te)
                    BK[ub].wrote(t_uT)
                    P.act(EQF[:, 0:CTA], BK[ub].pb[0:128, 0:CTA], AF.Exp, scale=1.0,
                          wait=[t_uT] + ([eqf_free] if eqf_free else []))
                    t_eq = P.act(EQF[:, CTA:2 * CTA], BK[ub].pb[0:128, 0:CTA], AF.Exp,
                                 scale=-1.0, inc=ac)
                    BK[ub].read(t_eq)
                    ab = 4 + 3 * (h % 2)
                    w0 = [t_eq, t_kvr] + BK[ab].wwait()
                    P.mm(BK[ab].pb[r0:r0 + 64, 0:CTA], KV0[:, 65 * h:65 * h + 64],
                         EQF[:, 0:CTA], start=True, stop=False, wait=w0)
                    P.mm(BK[ab].pb[r0:r0 + 64, 0:CTA],
                         KV0[:, 65 * (16 + h):65 * (16 + h) + 64],
                         EQF[:, CTA:2 * CTA], start=False, stop=True)
                    P.mm(BK[ab].pb[0:1, CTA:2 * CTA], KV0[:, 65 * h + 64:65 * h + 65],
                         EQF[:, 0:CTA], start=True, stop=False)
                    t_ab2 = P.mm(BK[ab].pb[0:1, CTA:2 * CTA],
                                 KV0[:, 65 * (16 + h) + 64:65 * (16 + h) + 65],
                                 EQF[:, CTA:2 * CTA], start=False, stop=True, inc=te)
                    BK[ab].wrote(t_ab2)
                    eqf_free = t_ab2
                    t_evA = P.act(NUMF[r0:r0 + 64, hp * CTA:(hp + 1) * CTA],
                                  BK[ab].pb[r0:r0 + 64, 0:CTA], AF.Identity,
                                  wait=[t_ab2], inc=ac)
                    t_evB = P.cp("vector", BSTG,
                                 BK[ab].pb[0:1, CTA:2 * CTA],
                                 wait=[t_ab2, t_evA], inc=ve)
                    BK[ab].read(t_evA)
                    BK[ab].read(t_evB)
                    for tl in range(NTIL_A):
                        t_tpb = P.tp(BK[5].pb[0:128, 32 + tl * 2:33 + tl * 2],
                                     STA[0:1, 384 + tl * 128:384 + (tl + 1) * 128],
                                     ONES[0:1, 0:128], 1,
                                     wait=[t_evB] + BK[5].wwait(), inc=te)
                        BK[5].wrote(t_tpb)
                        t_btc = P.cp("vector", BTT[:, 16 * tl + h:16 * tl + h + 1],
                                     BK[5].pb[0:128, 32 + tl * 2:33 + tl * 2],
                                     wait=[t_tpb], inc=ve)
                        BK[5].read(t_btc)
                t_evs = [(ac, ac.n), (ve, ve.n)]
                t_rd = None
                for tl in range(NTIL_A):
                    P.tt(DENT, EWTM[:, 16 * tl:16 * tl + 16], EHB, ALU.mult,
                         wait=t_evs + [t_eh, t_ewr])
                    P.tt(DENT, DENT, BTT[:, 16 * tl:16 * tl + 16], ALU.add)
                    t_dn = P.recip(DENT, DENT, inc=ve)
                    t_tpr = P.tp(BK[5].pb[0:16, 128:256], DENT, IDENT, 128,
                                 wait=[t_dn] + BK[5].wwait(), inc=te)
                    BK[5].wrote(t_tpr)
                    t_rd = P.cp("vector", RDEN[0:16, tl * 128:(tl + 1) * 128],
                                BK[5].pb[0:16, 128:256], wait=[t_tpr], inc=ve)
                    BK[5].read(t_rd)
                for dc in range(8):
                    bb = 6 + (dc % 2)
                    t_bc = P.mm(BK[bb].pb[0:128, 0:CTA], INDB[0:16, dc * 128:(dc + 1) * 128],
                                RDEN, wait=[t_rd, t_indb] + BK[bb].wwait(), inc=te)
                    BK[bb].wrote(t_bc)
                    t_sc = P.tt(NUMF[:, dc * CTA:(dc + 1) * CTA],
                                NUMF[:, dc * CTA:(dc + 1) * CTA],
                                BK[bb].pb[0:128, 0:CTA], ALU.mult, wait=[t_bc], inc=ve)
                    BK[bb].read(t_sc)
                t_num = (ve, ve.n)
                for oc in range(8):
                    sl = oc % 2
                    w_ap = bass.AP(tensor=wout0.tensor, offset=oc * 128,
                                   ap=[[D, 128], [128 * D, 8], [1, 128]])
                    t_wo = P.dma("sync",
                                 wo_slot[sl].rearrange("p (c j) -> p c j", j=128),
                                 w_ap, ld, wait=[wo_last[sl]] if wo_last[sl] else [])
                    bk = BK[sl]
                    for dc in range(8):
                        w0 = ([t_wo, t_num] + bk.wwait()) if dc == 0 else []
                        t_o = P.mm(bk.pb[0:128, 0:CTA],
                                   wo_slot[sl][:, dc * 128:(dc + 1) * 128],
                                   NUMF[:, dc * CTA:(dc + 1) * CTA],
                                   start=(dc == 0), stop=(dc == 7), wait=w0,
                                   inc=te if dc == 7 else None)
                    bk.wrote(t_o)
                    t_ra = P.tt(RES[:, oc * T + cols.start:oc * T + cols.stop],
                                RES[:, oc * T + cols.start:oc * T + cols.stop],
                                bk.pb[0:128, 0:CTA], ALU.add, wait=[t_o], inc=ve)
                    bk.read(t_ra)
                    wo_last[sl] = t_ra

        resid_ready = (ve, ve.n)

    if "l0f" in blocks:
        CTF = min(256, T)
        NCHF = T // CTF
        NFB = 32

        P.phase("l0fp")
        t_b1 = P.dma("sync", B1S, bass.AP(tensor=b1.tensor, offset=0,
                                          ap=[[1, 128], [128, 32]]), ld, slow=True)
        t_b2 = P.dma("sync", B2S, bass.AP(tensor=b2.tensor, offset=0,
                                          ap=[[1, 128], [128, 8]]), ld, slow=True)
        wsl = [WS[:, 0:1024], WS[:, 1024:2048]]
        QS = WS[:, 2048:3072]
        TSG = [WS[:, 3072:3456], WS[:, 3456:3840]]

        def w_chunk_ap(w, rc, rowlen):
            r, cb = rc // (rowlen // 1024), rc % (rowlen // 1024)
            return bass.AP(tensor=w.tensor, offset=r * 128 * rowlen + cb * 1024,
                           ap=[[rowlen, 128], [1, 1024]])

        def gamma_of(w, rowlen, nchunk, rg_out):
            fence = [(ve, ve.n), (ac, ac.n), (te, te.n)]
            prev = [fence[0], fence[0]]
            P.ph["lastwait"]
            prev = [None, None]
            first_extra = fence
            t_ab = None
            for rc in range(nchunk):
                sl = rc % 2
                t_w = P.dma("sync", wsl[sl], w_chunk_ap(w, rc, rowlen), ld,
                            wait=([prev[sl]] if prev[sl] else first_extra))
                t_ab = P.act(QS, wsl[sl], AF.Abs, accum_out=GAB[:, rc % 32:rc % 32 + 1],
                             wait=[t_w], inc=ac)
                prev[sl] = t_ab
            t_gs = P.mm(BK[5].pb[0:1, 0:32], ONES[:, 0:1], GAB,
                        wait=[t_ab] + BK[5].wwait(), inc=te)
            BK[5].wrote(t_gs)
            t_red = P.red(LNR[0:1, 0:1], BK[5].pb[0:1, 0:32], ALU.add, wait=[t_gs], inc=ve)
            BK[5].read(t_red)
            P.ts(LNR[0:1, 0:1], LNR[0:1, 0:1], float(1.0 / 4194304.0), 1e-5,
                 ALU.mult, ALU.add)
            t_g = P.recip(LNR[0:1, 1:2], LNR[0:1, 0:1], inc=ve)
            t_gb = P.mm(BK[5].pb[0:128, 32:33], ONES[0:1, 0:128], LNR[0:1, 1:2],
                        wait=[t_g] + BK[5].wwait(), inc=te)
            BK[5].wrote(t_gb)
            t_rg = P.cp("vector", rg_out, BK[5].pb[0:128, 32:33], wait=[t_gb], inc=ve)
            BK[5].read(t_rg)
            return t_rg

        def quant_store(w, rowlen, nchunk, rg, sc, dst_col):
            fence = [(ve, ve.n), (ac, ac.n), (te, te.n)]
            prev = [None, None]
            tsg_free = [None, None]
            t_done = None
            for rc in range(nchunk):
                sl = rc % 2
                t_w = P.dma("sync", wsl[sl], w_chunk_ap(w, rc, rowlen), ld,
                            wait=([prev[sl]] if prev[sl] else fence))
                P.ts(QS, wsl[sl], rg, None, ALU.mult, wait=[t_w, (te, te.n)])
                P.ts(XNKQ[:, 0:1024], QS, C32, C32, ALU.add, ALU.subtract)
                P.ts(QS, QS, C64, C64, ALU.add, ALU.subtract)
                P.stt(QS, XNKQ[:, 0:1024], GAM, QS, ALU.mult, ALU.add)
                t_q = P.ts(QS, QS, 1.0, -1.0, ALU.min, ALU.max, inc=ve)
                prev[sl] = t_q
                for blk in range(8):
                    tb = blk % 2
                    t_tp = P.tp(BK[tb].pb[0:128, 0:128], QS[:, blk * 128:(blk + 1) * 128],
                                IDENT, 128, wait=[t_q] + BK[tb].wwait(), inc=te)
                    BK[tb].wrote(t_tp)
                    t_ev = P.act(TSG[tb][:, 0:128], BK[tb].pb[0:128, 0:128], AF.Identity,
                                 wait=[t_tp] + ([tsg_free[tb]] if tsg_free[tb] else []),
                                 inc=ac)
                    BK[tb].read(t_ev)
                    t_done = P.dma("gpsimd", sc[:, dst_col(rc, blk):dst_col(rc, blk) + 128],
                                   TSG[tb][:, 0:128], st, wait=[t_ev])
                    tsg_free[tb] = t_done
            return t_done

        t_rg1 = gamma_of(w1, 1024, 32, RG1)
        t_rg2 = gamma_of(w2, 4096, 32, RG2)
        t_s1 = quant_store(w1, 1024, 32, RG1, wq1t_sc,
                           lambda rc, blk: rc * 1024 + blk * 128)
        t_s2 = quant_store(w2, 4096, 32, RG2, wq2t_sc,
                           lambda rc, blk: ((rc % 4) * 8 + blk) * 1024 + (rc // 4) * 128)

        P.phase("l0f")
        XNF = XNKQ[:, 0:8 * CTF]
        HF = [UA[:, 0:CTF], UA[:, CTF:2 * CTF]]
        w1sl = [WS[:, 0:1024], WS[:, 1024:2048]]
        w2sl = [WS[:, 2048:3072], WS[:, 3072:4096]]
        w1_prev = [t_s1, t_s1]
        w2_prev = [t_s2, t_s2]
        hf_free = [None, None]
        for ch in range(NCHF):
            cols = slice(ch * CTF, (ch + 1) * CTF)
            t_xn = ln_chunk(cols, XNF, 6, 6, resid_ready)
            t_oacc = None
            for fb in range(NFB):
                sl = fb % 2
                t_w1 = P.dma("sync", w1sl[sl], wq1t_sc[:, fb * 1024:(fb + 1) * 1024],
                             ld, wait=[w1_prev[sl]] if w1_prev[sl] else [])
                t_w2 = P.dma("sync", w2sl[sl], wq2t_sc[:, fb * 1024:(fb + 1) * 1024],
                             ld, wait=[w2_prev[sl]] if w2_prev[sl] else [])
                hb = BK[fb % 2]
                for dc in range(8):
                    w0 = ([t_w1, t_xn] + hb.wwait()) if dc == 0 else []
                    t_h = P.mm(hb.pb[0:128, 0:CTF],
                               w1sl[sl][:, dc * 128:(dc + 1) * 128],
                               XNF[:, dc * CTF:(dc + 1) * CTF],
                               start=(dc == 0), stop=(dc == 7), wait=w0,
                               inc=te if dc == 7 else None)
                hb.wrote(t_h)
                w1_prev[sl] = t_h
                t_hf = P.act(HF[fb % 2], hb.pb[0:128, 0:CTF], AF.Relu,
                             bias=B1S[:, fb:fb + 1], scale=1.0,
                             wait=[t_h, t_b1] + ([hf_free[fb % 2]] if hf_free[fb % 2] else []),
                             inc=ac)
                hb.read(t_hf)
                for oc in range(8):
                    ob = BK[2 + oc // 2]
                    w0 = [t_hf, t_w2]
                    if fb == 0 and oc % 2 == 0:
                        w0 = w0 + ob.wwait()
                    t_oacc = P.mm(ob.pb[0:128, (oc % 2) * CTF:(oc % 2 + 1) * CTF],
                                  w2sl[sl][:, oc * 128:(oc + 1) * 128], HF[fb % 2],
                                  start=(fb == 0 and oc % 2 == 0),
                                  stop=(fb == NFB - 1 and oc % 2 == 1),
                                  wait=w0, inc=te if (oc == 7) else None)
                    if fb == NFB - 1 and oc % 2 == 1:
                        ob.wrote(t_oacc)
                hf_free[fb % 2] = t_oacc
                w2_prev[sl] = t_oacc
            for oc in range(8):
                ob = BK[2 + oc // 2]
                t_ra = P.stt(RES[:, oc * T + cols.start:oc * T + cols.stop],
                             ob.pb[0:128, (oc % 2) * CTF:(oc % 2 + 1) * CTF],
                             B2S[:, oc:oc + 1],
                             RES[:, oc * T + cols.start:oc * T + cols.stop],
                             ALU.add, ALU.add, wait=[t_oacc, t_b2], inc=ve)
                ob.read(t_ra)
        resid_ready = (ve, ve.n)

    if "l1a" in blocks:
        NSEQ = 2 * T

        P.phase("l1p")
        t_qb = P.dma("sync", QBS, bass.AP(tensor=qkvb1.tensor, offset=0,
                                          ap=[[1, 128], [128, 24]]), ld, slow=True)
        t_pb = P.dma("sync", PJBS, bass.AP(tensor=projb1.tensor, offset=0,
                                           ap=[[1, 128], [128, 8]]), ld, slow=True)
        t_qbv = P.dma("sync", QBV, bass.AP(tensor=qkvb1.tensor, offset=2048,
                                           ap=[[1024, 1], [1, 1024]]), ld)
        t_ih = P.dma("sync", INDH[0:16, :], indh_h[:, :], ld)
        wslp = [WS[:, 0:1024], WS[:, 1024:2048]]
        TSG2 = [WS[:, 2048:2176], WS[:, 2176:2304]]

        def transp_store(w, nchunk, rowlen, sc, dst_col):
            fence = [(ve, ve.n), (ac, ac.n), (te, te.n)]
            prev = [None, None]
            tsg_free = [None, None]
            t_done = None
            for rc in range(nchunk):
                sl = rc % 2
                t_w = P.dma("sync", wslp[sl], bass.AP(tensor=w.tensor,
                                                      offset=rc * 128 * rowlen,
                                                      ap=[[rowlen, 128], [1, 1024]]),
                            ld, wait=([prev[sl]] if prev[sl] else fence))
                for blk in range(8):
                    tb = blk % 2
                    t_tp = P.tp(BK[tb].pb[0:128, 0:128],
                                wslp[sl][:, blk * 128:(blk + 1) * 128],
                                IDENT, 128, wait=[t_w] + BK[tb].wwait(), inc=te)
                    BK[tb].wrote(t_tp)
                    prev[sl] = t_tp
                    t_ev = P.act(TSG2[tb], BK[tb].pb[0:128, 0:128], AF.Identity,
                                 wait=[t_tp] + ([tsg_free[tb]] if tsg_free[tb] else []),
                                 inc=ac)
                    BK[tb].read(t_ev)
                    t_done = P.dma("gpsimd", sc[:, dst_col(rc, blk):dst_col(rc, blk) + 128],
                                   TSG2[tb], st, wait=[t_ev])
                    tsg_free[tb] = t_done
            return t_done

        t_tq = transp_store(qkvw1, 24, 1024, qkv1t_sc,
                            lambda rc, blk: blk * 3072 + rc * 128)
        t_tp2 = transp_store(projw1, 8, 1024, proj1t_sc,
                             lambda rc, blk: blk * 1024 + rc * 128)
        t_pm1 = P.dma("sync", WS[0:64, 2304:2368], pm1[:, :], ld,
                      wait=[(te, te.n), (ac, ac.n)])
        t_tpm1 = P.tp(BK[0].pb[0:64, 0:64], WS[0:64, 2304:2368], IDENT, 64,
                      wait=[t_pm1] + BK[0].wwait(), inc=te)
        BK[0].wrote(t_tpm1)
        t_pm1a = P.act(PMT1[0:64, :], BK[0].pb[0:64, 0:64], AF.Identity, scale=0.125,
                       wait=[t_tpm1], inc=ac)
        BK[0].read(t_pm1a)
        t_pm1b = P.dma("gpsimd", PMT1[64:128, :], PMT1[0:64, :], st, wait=[t_pm1a])

        P.phase("l1a")
        P.memset(bass.AP(tensor=KV0, offset=64, ap=[[2080, 64], [65, 16]]), 0.0, inc=gp)
        P.memset(XSUM, 0.0, inc=gp)
        t_init1 = (gp, gp.n)
        kv_open1 = [False, False]
        ksr_open = [False, False]
        t_kv_last1 = [None, None]
        t_ksr_last = [None, None]
        ekt_free = [None, None]
        for ch in range(NCH_A):
            cols = slice(ch * CTA, (ch + 1) * CTA)
            t_xn = ln_chunk(cols, XN, 5, 5, resid_ready)
            for kc in range(8):
                w_ap = bass.AP(tensor=qkv1t_sc, offset=(D + kc * 128),
                               ap=[[8 * 3072, 128], [3072, 8], [1, 128]])
                t_wk = P.dma("sync", WS[:, 0:1024].rearrange("p (c j) -> p c j", j=128),
                             w_ap, ld, wait=[(ac, ac.n), (te, te.n)] if kc == 0 else [(te, te.n)])
                bk = BK[0]
                for dc in range(8):
                    w0 = ([t_wk, t_xn, t_tq] + bk.wwait()) if dc == 0 else []
                    t_kmm = P.mm(bk.pb[0:128, 0:CTA],
                                 WS[:, dc * 128:(dc + 1) * 128],
                                 XN[:, dc * CTA:(dc + 1) * CTA],
                                 start=(dc == 0), stop=(dc == 7), wait=w0, inc=te)
                bk.wrote(t_kmm)
                t_kev = P.act(KQ[:, kc * CTA:(kc + 1) * CTA], bk.pb[0:128, 0:CTA],
                              AF.Identity, bias=QBS[:, 8 + kc:9 + kc], scale=1.0,
                              wait=[t_kmm, t_qb], inc=ac)
                bk.read(t_kev)
            t_k = (ac, ac.n)
            for vg in range(4):
                w_ap = bass.AP(tensor=qkv1t_sc, offset=(2 * D + vg * 256),
                               ap=[[8 * 3072, 128], [3072, 8], [1, 256]])
                t_wv = P.dma("sync",
                             WS[:, 2048:4096].rearrange("p (c j) -> p c j", j=256),
                             w_ap, ld, wait=[(te, te.n)])
                for tl in range(NTIL_A):
                    bk = BK[0]
                    for dc in range(8):
                        w0 = ([t_wv, t_xn] + bk.wwait()) if dc == 0 else []
                        t_vmm = P.mm(bk.pb[0:128, 0:256],
                                     XN[:, dc * CTA + tl * 128: dc * CTA + (tl + 1) * 128],
                                     WS[:, 2048 + dc * 256:2048 + (dc + 1) * 256],
                                     start=(dc == 0), stop=(dc == 7), wait=w0,
                                     inc=te if dc == 7 else None)
                    bk.wrote(t_vmm)
                    t_vev = P.act(VT[:, tl * 1024 + vg * 256: tl * 1024 + (vg + 1) * 256],
                                  bk.pb[0:128, 0:256], AF.Identity, wait=[t_vmm], inc=ac)
                    bk.read(t_vev)
            t_v = (ac, ac.n)
            for dc in range(8):
                P.red(SCRR, XN[:, dc * CTA:(dc + 1) * CTA], ALU.add, wait=[t_xn])
                P.tt(XSUM[:, dc:dc + 1], XSUM[:, dc:dc + 1], SCRR, ALU.add,
                     wait=[t_init1], inc=ve)
            for tl in range(NTIL_A):
                for h in range(H):
                    hp, r0 = h // 2, 64 * (h % 2)
                    ub = 2 + (h % 2)
                    t_g = P.mm(BK[ub].pb[0:128, 0:64],
                               KQ[r0:r0 + 64, hp * CTA + tl * 128: hp * CTA + (tl + 1) * 128],
                               PMT1[r0:r0 + 64, 0:64],
                               wait=[t_k, t_v, t_pm1b] + BK[ub].wwait(), inc=te)
                    BK[ub].wrote(t_g)
                    t_rm = P.red(SCRR, BK[ub].pb[0:128, 0:64], ALU.max, wait=[t_g], inc=ve)
                    BK[ub].read(t_rm)
                    t_nr = P.ts(SCR1[:, 0:1], SCRR, -1.0, float(-np.log(8.0)),
                                ALU.mult, ALU.add, inc=ve)
                    es = h % 2
                    t_ek = P.act(EKT[:, es * 64:es * 64 + 64], BK[ub].pb[0:128, 0:64],
                                 AF.Exp, bias=SCR1[:, 0:1], scale=1.0,
                                 wait=[t_nr, t_g] + ([ekt_free[es]] if ekt_free[es] else []),
                                 inc=ac)
                    BK[ub].read(t_ek)
                    kb = 1 if h < 8 else 4
                    kbi = 0 if h < 8 else 1
                    stt0 = not kv_open1[kbi]
                    kv_open1[kbi] = True
                    is_last = (ch == NCH_A - 1 and tl == NTIL_A - 1 and (h % 8) == 7)
                    t_kv = P.mm(PB[kb][0:64, (h % 8) * 64:(h % 8) * 64 + 64],
                                EKT[:, es * 64:es * 64 + 64],
                                VT[:, tl * 1024 + h * 64: tl * 1024 + (h + 1) * 64],
                                start=stt0, stop=is_last, wait=[t_ek], inc=te)
                    t_kv_last1[kbi] = t_kv
                    ekt_free[es] = t_kv
                    t_ks = P.mm(BK[ub].pb[0:64, 256 + (h // 2):257 + (h // 2)],
                                EKT[:, es * 64:es * 64 + 64], ONES[:, 0:1],
                                start=True, stop=True, wait=[t_ek], inc=te)
                    BK[ub].wrote(t_ks)
                    t_ka = P.tt(KV1[0:64, 65 * h + 64:65 * h + 65],
                                KV1[0:64, 65 * h + 64:65 * h + 65],
                                BK[ub].pb[0:64, 256 + (h // 2):257 + (h // 2)],
                                ALU.add, wait=[t_ks, t_init1], inc=ve)
                    BK[ub].read(t_ka)
                    rb = 6 if h < 8 else 7
                    rbi = 0 if h < 8 else 1
                    str0 = not ksr_open[rbi]
                    ksr_open[rbi] = True
                    t_kr = P.mm(PB[rb][0:1, (h % 8) * 64:(h % 8) * 64 + 64],
                                ONES[:, 0:1], EKT[:, es * 64:es * 64 + 64],
                                start=str0, stop=is_last, inc=te)
                    t_ksr_last[rbi] = t_kr
                    ekt_free[es] = t_kr
        t_kvev1 = None
        for i, kb in enumerate((1, 4)):
            out_ap = bass.AP(tensor=KV0, offset=65 * 8 * i,
                             ap=[[2080, 64], [65, 8], [1, 64]])
            in_ap = PB[kb][0:64, 0:512].rearrange("p (b j) -> p b j", j=64)
            t_kvev1 = P.cp("vector", out_ap, in_ap, wait=[t_kv_last1[i]], inc=ve)
            BK[kb].read(t_kvev1)
            BK[kb].wtok = None
        t_rowout = None
        for i, rb in enumerate((6, 7)):
            t_kr2 = P.cp("vector", LNR[0:1, 0:512], PB[rb][0:1, 0:512],
                         wait=[t_ksr_last[i], t_rowout] if t_rowout else [t_ksr_last[i]],
                         inc=ve)
            BK[rb].read(t_kr2)
            BK[rb].wtok = None
            t_rowout = P.dma("gpsimd", row_bi[0:1, i * 512:(i + 1) * 512],
                             LNR[0:1, 0:512], st, wait=[t_kr2])
        for vg in range(4):
            w_ap = bass.AP(tensor=qkv1t_sc, offset=(2 * D + vg * 256),
                           ap=[[8 * 3072, 128], [3072, 8], [1, 256]])
            t_wv2 = P.dma("sync", WS[:, 2048:4096].rearrange("p (c j) -> p c j", j=256),
                          w_ap, ld, wait=[(te, te.n)])
            for dc in range(8):
                w0 = ([t_wv2, (ve, ve.n)] + BK[5].wwait()) if dc == 0 else []
                t_vs = P.mm(BK[5].pb[0:1, (vg % 2) * 256:(vg % 2) * 256 + 256],
                            XSUM[:, dc:dc + 1],
                            WS[:, 2048 + dc * 256:2048 + (dc + 1) * 256],
                            start=(dc == 0), stop=(dc == 7), wait=w0, inc=te)
            BK[5].wrote(t_vs)
            t_ve2 = P.cp("vector", LNR[0:1, (vg % 2) * 256:(vg % 2) * 256 + 256],
                         BK[5].pb[0:1, (vg % 2) * 256:(vg % 2) * 256 + 256],
                         wait=[t_vs, t_rowout], inc=ve)
            BK[5].read(t_ve2)
            if vg % 2 == 1:
                t_rowout = P.dma("gpsimd", row_bi[0:1, 1024 + (vg // 2) * 512:
                                                   1024 + (vg // 2 + 1) * 512],
                                 LNR[0:1, 0:512], st, wait=[t_ve2])
        t_s1 = P.dma("gpsimd", kv1_bi[:, :], KV1[:, :], st, wait=[t_kvev1, (ve, ve.n)])
        pairs_g = [[2 * i, 2 * i + 1] for i in range(n_cores // 2)]
        t_c1 = P.coll("AllReduce", ALU.add, pairs_g, [kv1_bi[:, :].opt()],
                      [kv1_bo[:, :].opt()], wait=[t_s1], inc=cc)
        t_c2 = P.coll("AllReduce", ALU.add, pairs_g, [row_bi[:, :].opt()],
                      [row_bo[:, :].opt()], wait=[t_rowout], inc=cc)
        t_kv1 = P.dma("gpsimd", KV1[:, :], kv1_bo[:, :], st, wait=[t_c1])
        t_ksr = P.dma("gpsimd", KSR, row_bo[0:1, 0:1024], st, wait=[t_c2])
        t_vsf = t_ksr
        for k2 in range(2):
            t_vl = P.dma("sync", LNR[0:1, 0:512], row_bo[0:1, 1024 + k2 * 512:
                                                         1024 + (k2 + 1) * 512],
                         ld, wait=[t_c2, (ve, ve.n), t_vsf])
            t_bv = P.dma("sync", SQS[0:1, 0:512],
                         bass.AP(tensor=qkvb1.tensor, offset=2048 + k2 * 512,
                                 ap=[[512, 1], [1, 512]]), ld, wait=[(ve, ve.n)])
            P.stt(LNR[0:1, 0:512], SQS[0:1, 0:512], float(NSEQ), LNR[0:1, 0:512],
                  ALU.mult, ALU.add, wait=[t_vl, t_bv])
            t_v0 = P.ts(LNR[0:1, 0:512], LNR[0:1, 0:512], EPS0, None, ALU.mult, inc=ve)
            t_vsf = P.dma("gpsimd", KV0[65:66, 1040 + k2 * 512:1040 + (k2 + 1) * 512],
                          LNR[0:1, 0:512], st, wait=[t_v0])
        t_o65 = None
        for k8 in range(8):
            t_o65 = P.dma("gpsimd", ONE65[0:1, k8 * 128:(k8 + 1) * 128],
                          ONES[64:65, 0:128], st, wait=[t_ones])
        t_fix = None
        for h in range(H):
            fb = BK[5]
            t_r1 = P.mm(fb.pb[0:64, 0:64], KV0[64:66, 64 * h:64 * h + 64],
                        KV0[64:66, 1040 + 64 * h:1040 + 64 * h + 64],
                        start=True, stop=True,
                        wait=[t_vsf, t_kv1, t_qbv, t_o65] + fb.wwait(), inc=te)
            fb.wrote(t_r1)
            t_fix = P.tt(KV1[0:64, 65 * h:65 * h + 64], KV1[0:64, 65 * h:65 * h + 64],
                         fb.pb[0:64, 0:64], ALU.add, wait=[t_r1], inc=ve)
            fb.read(t_fix)
        t_kfix = P.ts(bass.AP(tensor=KV0, offset=64, ap=[[2080, 64], [65, 16]]),
                      bass.AP(tensor=KV0, offset=64, ap=[[2080, 64], [65, 16]]),
                      float(EPS0 * NSEQ), None, ALU.add, wait=[t_fix], inc=ve)
        ks_ap = bass.AP(tensor=KV0, offset=64, ap=[[2080, 64], [65, 16]])
        t_tm = P.mm(BK[5].pb[0:16, 0:1], ks_ap, ONES[0:64, 0:1],
                    wait=[t_kfix] + BK[5].wwait(), inc=te)
        BK[5].wrote(t_tm)
        t_tv = P.ts(STA[0:16, 89:90], BK[5].pb[0:16, 0:1], EPS0, 1e-6,
                    ALU.mult, ALU.add, wait=[t_tm], inc=ve)
        BK[5].read(t_tv)
        t_tpt = P.tp(BK[5].pb[0:1, 0:16], STA[0:16, 89:90], IDENT, 16,
                     wait=[t_tv] + BK[5].wwait(), inc=te)
        BK[5].wrote(t_tpt)
        t_tr = P.cp("vector", LNR[0:1, 0:16],
                    BK[5].pb[0:1, 0:16], wait=[t_tpt], inc=ve)
        BK[5].read(t_tr)
        t_etb0 = P.mm(BK[5].pb[0:128, 16:32], ONES[0:1, 0:128], LNR[0:1, 0:16],
                      wait=[t_tr] + BK[5].wwait(), inc=te)
        BK[5].wrote(t_etb0)
        t_etb = P.cp("vector", ETB, BK[5].pb[0:128, 16:32], wait=[t_etb0], inc=ve)
        BK[5].read(t_etb)
        t_esb = None
        for h in range(H):
            t_sm = P.mm(BK[5].pb[0:64, 32 + h:33 + h], KV1[0:64, 65 * h:65 * h + 64],
                        ONES[0:64, 0:1], wait=BK[5].wwait(), inc=te)
            BK[5].wrote(t_sm)
            t_e0 = P.ts(ESB[0:64, h:h + 1], BK[5].pb[0:64, 32 + h:33 + h],
                        EPS0, None, ALU.mult, wait=[t_sm], inc=ve)
            BK[5].read(t_e0)
        t_esb = P.dma("gpsimd", ESB2[64:128, 0:16], ESB[0:64, 0:16], st,
                      wait=[(ve, ve.n)])

        P.phase("l1b")
        eqf_free = None
        prj_last = [None, None]
        for ch in range(NCH_A):
            cols = slice(ch * CTA, (ch + 1) * CTA)
            t_xn = ln_chunk(cols, XN, 5, 5, resid_ready)
            for qc in range(8):
                w_ap = bass.AP(tensor=qkv1t_sc, offset=qc * 128,
                               ap=[[8 * 3072, 128], [3072, 8], [1, 128]])
                t_wq = P.dma("sync", WS[:, 0:1024].rearrange("p (c j) -> p c j", j=128),
                             w_ap, ld, wait=[(ac, ac.n), (te, te.n)])
                bk = BK[0]
                for dc in range(8):
                    w0 = ([t_wq, t_xn] + bk.wwait()) if dc == 0 else []
                    t_qmm = P.mm(bk.pb[0:128, 0:CTA],
                                 WS[:, dc * 128:(dc + 1) * 128],
                                 XN[:, dc * CTA:(dc + 1) * CTA],
                                 start=(dc == 0), stop=(dc == 7), wait=w0, inc=te)
                bk.wrote(t_qmm)
                t_qev = P.act(KQ[:, qc * CTA:(qc + 1) * CTA], bk.pb[0:128, 0:CTA],
                              AF.Identity, bias=QBS[:, qc:qc + 1], scale=1.0,
                              wait=[t_qmm], inc=ac)
                bk.read(t_qev)
            t_q = (ac, ac.n)
            for tl in range(NTIL_A):
                for h in range(H):
                    hp, r0 = h // 2, 64 * (h % 2)
                    ub = 2 + (h % 2)
                    t_g = P.mm(BK[ub].pb[0:128, 0:64],
                               KQ[r0:r0 + 64, hp * CTA + tl * 128: hp * CTA + (tl + 1) * 128],
                               PMT1[r0:r0 + 64, 0:64],
                               wait=[t_q] + BK[ub].wwait(), inc=te)
                    BK[ub].wrote(t_g)
                    t_rm = P.red(MQT[:, h:h + 1], BK[ub].pb[0:128, 0:64], ALU.max,
                                 wait=[t_g], inc=ve)
                    BK[ub].read(t_rm)
                t_ml8 = P.ts(MQT, MQT, -1.0, float(-np.log(8.0)), ALU.mult, ALU.add,
                             inc=ve)
                t_tpr = P.tp(BK[5].pb[0:16, 0:128], MQT, IDENT, 128,
                             wait=[t_ml8] + BK[5].wwait(), inc=te)
                BK[5].wrote(t_tpr)
                t_rmx = P.cp("vector", RMXR[0:16, tl * 128:(tl + 1) * 128],
                             BK[5].pb[0:16, 0:128], wait=[t_tpr], inc=ve)
                BK[5].read(t_rmx)
            t_rmr = (ve, ve.n)
            for h in range(H):
                hp, r0 = h // 2, 64 * (h % 2)
                ub = 2 + (h % 2)
                P.mm(BK[ub].pb[0:64, 0:CTA], PMT1[r0:r0 + 64, 0:64],
                     KQ[r0:r0 + 64, hp * CTA: hp * CTA + CTA],
                     start=True, stop=False,
                     wait=[t_rmr, t_ih] + BK[ub].wwait())
                t_gq = P.mm(BK[ub].pb[0:64, 0:CTA], INDH[0:16, h * 64:(h + 1) * 64],
                            RMXR, start=False, stop=True, inc=te)
                BK[ub].wrote(t_gq)
                t_eq = P.act(EQF[0:64, CTA:2 * CTA], BK[ub].pb[0:64, 0:CTA], AF.Exp,
                             wait=[t_gq] + ([eqf_free] if eqf_free else []), inc=ac)
                BK[ub].read(t_eq)
                ab = 4 + 3 * (h % 2)
                w0 = [t_eq] + BK[ab].wwait()
                P.mm(BK[ab].pb[r0:r0 + 64, 0:CTA], KV1[0:64, 65 * h:65 * h + 64],
                     EQF[0:64, CTA:2 * CTA], start=True, stop=True, wait=w0)
                t_ab2 = P.mm(BK[ab].pb[0:1, CTA:2 * CTA],
                             KV1[0:64, 65 * h + 64:65 * h + 65],
                             EQF[0:64, CTA:2 * CTA], start=True, stop=True, inc=te)
                BK[ab].wrote(t_ab2)
                eqf_free = t_ab2
                esb_t = ESB if r0 == 0 else ESB2
                t_evA = P.act(NUMF[r0:r0 + 64, hp * CTA:(hp + 1) * CTA],
                              BK[ab].pb[r0:r0 + 64, 0:CTA], AF.Identity,
                              bias=esb_t[r0:r0 + 64, h:h + 1], scale=1.0,
                              wait=[t_ab2, t_esb], inc=ac)
                t_evB = P.cp("vector", BSTG,
                             BK[ab].pb[0:1, CTA:2 * CTA], wait=[t_ab2, t_evA], inc=ve)
                BK[ab].read(t_evA)
                BK[ab].read(t_evB)
                for tl in range(NTIL_A):
                    t_tpb = P.tp(BK[5].pb[0:128, 32 + tl * 2:33 + tl * 2],
                                 STA[0:1, 384 + tl * 128:384 + (tl + 1) * 128],
                                 ONES[0:1, 0:128], 1,
                                 wait=[t_evB] + BK[5].wwait(), inc=te)
                    BK[5].wrote(t_tpb)
                    t_btc = P.cp("vector", BTT[:, 16 * tl + h:16 * tl + h + 1],
                                 BK[5].pb[0:128, 32 + tl * 2:33 + tl * 2],
                                 wait=[t_tpb], inc=ve)
                    BK[5].read(t_btc)
            t_evs = [(ac, ac.n), (ve, ve.n)]
            t_rd = None
            for tl in range(NTIL_A):
                P.tt(DENT, BTT[:, 16 * tl:16 * tl + 16], ETB, ALU.add,
                     wait=t_evs + [t_etb])
                t_dn = P.recip(DENT, DENT, inc=ve)
                t_tpr = P.tp(BK[5].pb[0:16, 128:256], DENT, IDENT, 128,
                             wait=[t_dn] + BK[5].wwait(), inc=te)
                BK[5].wrote(t_tpr)
                t_rd = P.cp("vector", RDEN[0:16, tl * 128:(tl + 1) * 128],
                            BK[5].pb[0:16, 128:256], wait=[t_tpr], inc=ve)
                BK[5].read(t_rd)
            for dc in range(8):
                bb = 6 + (dc % 2)
                t_bcr = P.mm(BK[bb].pb[0:128, 0:CTA], INDB[0:16, dc * 128:(dc + 1) * 128],
                             RDEN, wait=[t_rd, t_indb] + BK[bb].wwait(), inc=te)
                BK[bb].wrote(t_bcr)
                t_sc = P.tt(NUMF[:, dc * CTA:(dc + 1) * CTA],
                            NUMF[:, dc * CTA:(dc + 1) * CTA],
                            BK[bb].pb[0:128, 0:CTA], ALU.mult, wait=[t_bcr], inc=ve)
                BK[bb].read(t_sc)
            t_num = (ve, ve.n)
            for oc in range(8):
                sl = oc % 2
                w_ap = bass.AP(tensor=proj1t_sc, offset=oc * 128,
                               ap=[[8 * 1024, 128], [1024, 8], [1, 128]])
                t_wo = P.dma("sync",
                             WS[:, 2048 + sl * 1024:2048 + (sl + 1) * 1024].rearrange(
                                 "p (c j) -> p c j", j=128),
                             w_ap, ld, wait=[prj_last[sl]] if prj_last[sl] else [(te, te.n)])
                bk = BK[sl]
                for dc in range(8):
                    w0 = ([t_wo, t_num, t_tp2] + bk.wwait()) if dc == 0 else []
                    t_o = P.mm(bk.pb[0:128, 0:CTA],
                               WS[:, 2048 + sl * 1024 + dc * 128:2048 + sl * 1024 + (dc + 1) * 128],
                               NUMF[:, dc * CTA:(dc + 1) * CTA],
                               start=(dc == 0), stop=(dc == 7), wait=w0, inc=te)
                bk.wrote(t_o)
                t_ra = P.stt(RES[:, oc * T + cols.start:oc * T + cols.stop],
                             bk.pb[0:128, 0:CTA], PJBS[:, oc:oc + 1],
                             RES[:, oc * T + cols.start:oc * T + cols.stop],
                             ALU.add, ALU.add, wait=[t_o, t_pb], inc=ve)
                bk.read(t_ra)
                prj_last[sl] = t_ra
        resid_ready = (ve, ve.n)

    if "l1f" in blocks:
        CT1 = min(256, T)
        NCH1 = T // CT1
        NFB1 = 24
        NST = (NFB1 * T) // 512

        S1B = STA[:, 352:353]
        RS1B = STA[:, 353:354]
        S2B = STA[:, 354:355]
        RS2B = STA[:, 355:356]
        S3B = STA[:, 356:357]
        RS3B = STA[:, 357:358]
        K3B = STA[:, 358:359]
        S4B = STA[:, 359:360]
        RS4B = STA[:, 360:361]
        C34B = STA[:, 361:362]
        K45B = STA[:, 362:363]
        S5B = STA[:, 363:364]
        C45B = STA[:, 364:365]
        R327 = STA[:, 365:366]
        R7 = STA[:, 366:367]
        HALF = STA[:, 367:368]
        MXP = STA[:, 368:369]
        MBC = STA[:, 369:370]

        P.phase("l1fp")
        P.memset(R327, 32767.0, inc=gp)
        P.memset(R7, 7.0, inc=gp)
        P.memset(HALF, 0.5, inc=gp)
        t_cst = (gp, gp.n)
        P.recip(R327, R327, wait=[t_cst])
        t_r7 = P.recip(R7, R7, inc=ve)
        wslp2 = [WS[:, 0:1024], WS[:, 1024:2048]]
        QS2 = WS[:, 2048:3072]
        TSG3 = [WS[:, 3072:3200], WS[:, 3200:3328]]
        SEG = GAB
        RSEG = BTT[:, 0:32]

        def lpbq_store(w, nchunk, rowlen, chunk_ap, sc, dst_col):
            fence = [(ve, ve.n), (ac, ac.n), (te, te.n)]
            prev = [None, None]
            tsg_free = [None, None]
            t_done = None
            for rc in range(nchunk):
                sl = rc % 2
                t_w = P.dma("sync", wslp2[sl], chunk_ap(rc), ld,
                            wait=([prev[sl]] if prev[sl] else fence))
                t_sg = P.red(SEG, wslp2[sl].rearrange("p (s w) -> p s w", w=32),
                             ALU.max, absval=True, wait=[t_w, t_r7, (te, te.n)], inc=ve)
                P.op("vector", lambda e, sl=sl: e.tensor_scalar(
                    SEG, SEG, R7, 1e-12, ALU.mult, ALU.max))
                P.recip(RSEG, SEG)
                for j in range(32):
                    P.op("vector", lambda e, sl=sl, j=j: e.tensor_scalar(
                        QS2[:, j * 32:(j + 1) * 32], wslp2[sl][:, j * 32:(j + 1) * 32],
                        RSEG[:, j:j + 1], None, ALU.mult))
                P.ts(XNKQ[:, 0:1024], QS2, C32, C32, ALU.add, ALU.subtract)
                P.ts(QS2, QS2, C64, C64, ALU.add, ALU.subtract)
                P.stt(QS2, XNKQ[:, 0:1024], GAM, QS2, ALU.mult, ALU.add)
                t_q = None
                for j in range(32):
                    t_q = P.op("vector", lambda e, j=j: e.tensor_scalar(
                        QS2[:, j * 32:(j + 1) * 32], QS2[:, j * 32:(j + 1) * 32],
                        SEG[:, j:j + 1], None, ALU.mult), inc=ve)
                prev[sl] = t_q
                for blk in range(8):
                    tb = blk % 2
                    t_tp = P.tp(BK[tb].pb[0:128, 0:128], QS2[:, blk * 128:(blk + 1) * 128],
                                IDENT, 128, wait=[t_q] + BK[tb].wwait(), inc=te)
                    BK[tb].wrote(t_tp)
                    t_ev = P.act(TSG3[tb], BK[tb].pb[0:128, 0:128], AF.Identity,
                                 wait=[t_tp] + ([tsg_free[tb]] if tsg_free[tb] else []),
                                 inc=ac)
                    BK[tb].read(t_ev)
                    t_done = P.dma("gpsimd", sc[:, dst_col(rc, blk):dst_col(rc, blk) + 128],
                                   TSG3[tb], st, wait=[t_ev])
                    tsg_free[tb] = t_done
            return t_done

        def rows_ap(w, rowlen):
            return lambda rc: bass.AP(tensor=w.tensor, offset=rc * 128 * rowlen,
                                      ap=[[rowlen, 128], [1, 1024]])

        def wd_ap(rc):
            r, cb = rc // 3, rc % 3
            return bass.AP(tensor=wd1.tensor, offset=r * 128 * 3072 + cb * 1024,
                           ap=[[3072, 128], [1, 1024]])

        t_wu = lpbq_store(wu1, 24, 1024, rows_ap(wu1, 1024), wut_sc,
                          lambda rc, blk: rc * 1024 + blk * 128)
        t_wg = lpbq_store(wg1, 24, 1024, rows_ap(wg1, 1024), wgt_sc,
                          lambda rc, blk: rc * 1024 + blk * 128)
        t_wd = lpbq_store(wd1, 24, 3072, wd_ap, wdt_sc,
                          lambda rc, blk: ((rc % 3) * 8 + blk) * 1024 + (rc // 3) * 128)

        def global_max(nvals, stage_tok, vals_fn):
            t_st = P.dma("gpsimd", mx_bi[0:1, 0:nvals], LNR[0:1, 32:32 + nvals], st,
                         wait=stage_tok)
            t_cc = P.coll("AllReduce", ALU.max, [list(range(n_cores))],
                          [mx_bi[0:1, 0:nvals].opt()], [mx_bo[0:1, 0:nvals].opt()],
                          wait=[t_st], inc=cc)
            return P.dma("gpsimd", LNR[0:1, 0:nvals], mx_bo[0:1, 0:nvals], st,
                         wait=[t_cc])

        def finish_partial(col):
            t_tp = P.tp(BK[5].pb[0:1, 0:128], MXP, IDENT, 128,
                        wait=[(ve, ve.n)] + BK[5].wwait(), inc=te)
            BK[5].wrote(t_tp)
            t_c = P.cp("vector", LNR[0:1, 128:256], BK[5].pb[0:1, 0:128],
                       wait=[t_tp], inc=ve)
            BK[5].read(t_c)
            return P.red(LNR[0:1, 32 + col:33 + col], LNR[0:1, 128:256], ALU.max,
                         wait=[t_c], inc=ve)

        def bcast_scalar(src_col, dst, wait):
            t_b = P.mm(BK[5].pb[0:128, 200:201], ONES[0:1, 0:128],
                       LNR[0:1, src_col:src_col + 1],
                       wait=wait + BK[5].wwait(), inc=te)
            BK[5].wrote(t_b)
            t_r = P.cp("vector", dst, BK[5].pb[0:128, 200:201], wait=[t_b], inc=ve)
            BK[5].read(t_r)
            return t_r

        P.phase("l1f0")
        P.memset(MXP, 0.0, inc=gp)
        t_mx0 = (gp, gp.n)
        XNF1 = XNKQ[:, 0:8 * CT1]
        for ch in range(NCH1):
            cols = slice(ch * CT1, (ch + 1) * CT1)
            t_xn = ln_chunk(cols, XNF1, 5, 5, resid_ready)
            t_r = P.red(SCRR, XNF1, ALU.max, absval=True, wait=[t_xn], inc=ve)
            P.tt(MXP, MXP, SCRR, ALU.max, wait=[t_mx0], inc=ve)
        t_p = finish_partial(0)
        t_m1 = global_max(1, [t_p], None)
        t_mb = bcast_scalar(0, MBC, [t_m1])
        P.op("vector", lambda e: e.tensor_scalar(S1B, MBC, R327, 1e-12, ALU.mult, ALU.max),
             wait=[t_mb])
        t_s1b = P.recip(RS1B, S1B, inc=ve)

        P.phase("l1f1")
        P.memset(MXP, 0.0)
        P.memset(STA[:, 370:371], 0.0, inc=gp)
        t_mx1 = (gp, gp.n)
        MXP2 = STA[:, 370:371]
        upg_free = [None, None]
        wu_prev = [t_wu, t_wu]
        wg_prev = [t_wg, t_wg]
        for ch in range(NCH1):
            cols = slice(ch * CT1, (ch + 1) * CT1)
            t_xn = ln_chunk(cols, XNF1, 5, 5, resid_ready)
            t_qx = None
            npc = (8 * CT1) // 512
            for pc in range(npc):
                seg = XNF1[:, pc * 512:(pc + 1) * 512]
                t_w = P.act(UA[:, 0:512], seg, AF.Identity, scale=RS1B,
                            wait=[t_xn, t_s1b, (ve, ve.n)], inc=ac)
                P.ts(SQS[:, 0:512], UA[:, 0:512], C32, C32, ALU.add, ALU.subtract,
                     wait=[t_w])
                P.ts(UA[:, 512:1024], UA[:, 0:512], C64, C64, ALU.add, ALU.subtract)
                t_qx = P.stt(seg, SQS[:, 0:512], GAM, UA[:, 512:1024],
                             ALU.mult, ALU.add, inc=ve)
            for fb in range(NFB1):
                sl = fb % 2
                t_w1 = P.dma("sync", WS[:, sl * 1024:(sl + 1) * 1024],
                             wut_sc[:, fb * 1024:(fb + 1) * 1024], ld,
                             wait=[wu_prev[sl]] if wu_prev[sl] else [])
                t_w2 = P.dma("sync", WS[:, 2048 + sl * 1024:2048 + (sl + 1) * 1024],
                             wgt_sc[:, fb * 1024:(fb + 1) * 1024], ld,
                             wait=[wg_prev[sl]] if wg_prev[sl] else [])
                for which, t_wx, wofs, scx, mxacc, spill in (
                        (0, t_w1, sl * 1024, S1B, MXP, upre_sc),
                        (1, t_w2, 2048 + sl * 1024, S1B, MXP2, gpre_sc)):
                    bk = BK[which]
                    for dc in range(8):
                        w0 = ([t_wx, t_qx] + bk.wwait()) if dc == 0 else []
                        t_u = P.mm(bk.pb[0:128, 0:CT1],
                                   WS[:, wofs + dc * 128:wofs + (dc + 1) * 128],
                                   XNF1[:, dc * CT1:(dc + 1) * CT1],
                                   start=(dc == 0), stop=(dc == 7), wait=w0, inc=te)
                    bk.wrote(t_u)
                    slot = UA[:, 1024 + which * 512:1024 + which * 512 + CT1]
                    t_ev = P.act(slot, bk.pb[0:128, 0:CT1], AF.Identity, scale=scx,
                                 wait=[t_u] + ([upg_free[which]] if upg_free[which] else []),
                                 inc=ac)
                    bk.read(t_ev)
                    t_rd = P.red(SCRR, slot, ALU.max, absval=True, wait=[t_ev], inc=ve)
                    P.tt(mxacc, mxacc, SCRR, ALU.max, wait=[t_mx1], inc=ve)
                    t_sp = P.dma("gpsimd", spill[:, fb * T + cols.start:fb * T + cols.stop],
                                 slot, st, wait=[t_ev, t_rd])
                    upg_free[which] = t_sp
                    if which == 0:
                        wu_prev[sl] = t_u
                    else:
                        wg_prev[sl] = t_u
        t_p2 = finish_partial(0)
        t_tp = P.tp(BK[5].pb[0:1, 0:128], MXP2, IDENT, 128,
                    wait=[(ve, ve.n)] + BK[5].wwait(), inc=te)
        BK[5].wrote(t_tp)
        t_c = P.cp("vector", LNR[0:1, 128:256], BK[5].pb[0:1, 0:128], wait=[t_tp], inc=ve)
        BK[5].read(t_c)
        t_p3 = P.red(LNR[0:1, 33:34], LNR[0:1, 128:256], ALU.max, wait=[t_c], inc=ve)
        t_m23 = global_max(2, [t_p2, t_p3], None)
        t_mb2 = bcast_scalar(0, MBC, [t_m23])
        P.op("vector", lambda e: e.tensor_scalar(S2B, MBC, R327, 1e-12, ALU.mult, ALU.max),
             wait=[t_mb2])
        P.recip(RS2B, S2B)
        t_mb3 = bcast_scalar(1, MBC, [t_m23])
        P.op("vector", lambda e: e.tensor_scalar(S3B, MBC, R327, 1e-12, ALU.mult, ALU.max),
             wait=[t_mb3])
        P.recip(RS3B, S3B)
        t_k3 = P.ts(K3B, S3B, float(2.0 ** -16), None, ALU.mult, inc=ve)

        P.phase("l1f2")
        P.memset(MXP, 0.0, inc=gp)
        t_mx2 = (gp, gp.n)
        sp_free = [None, None]
        for st_i in range(NST):
            sl = st_i % 2
            gsl = XNKQ[:, sl * 1024:sl * 1024 + 512]
            t_g = P.dma("sync", gsl, gpre_sc[:, st_i * 512:(st_i + 1) * 512], ld,
                        wait=[sp_free[sl]] if sp_free[sl] else [(ve, ve.n), (ac, ac.n)])
            t_w = P.act(UA[:, 0:512], gsl, AF.Identity, scale=RS3B,
                        wait=[t_g, t_k3, (ve, ve.n)], inc=ac)
            P.ts(SQS[:, 0:512], UA[:, 0:512], C32, C32, ALU.add, ALU.subtract, wait=[t_w])
            P.ts(UA[:, 512:1024], UA[:, 0:512], C64, C64, ALU.add, ALU.subtract)
            t_qg = P.stt(gsl, SQS[:, 0:512], GAM, UA[:, 512:1024], ALU.mult, ALU.add,
                         inc=ve)
            P.act(UA[:, 0:512], gsl, AF.Sigmoid, scale=S3B, wait=[t_qg])
            t_w3 = P.act(UA[:, 512:1024], UA[:, 0:512], AF.Identity, scale=65536.0,
                         inc=ac)
            P.ts(SQS[:, 0:512], UA[:, 512:1024], C32, C32, ALU.add, ALU.subtract,
                 wait=[t_w3])
            P.ts(UA[:, 1024:1536], UA[:, 512:1024], C64, C64, ALU.add, ALU.subtract)
            P.stt(UA[:, 512:1024], SQS[:, 0:512], GAM, UA[:, 1024:1536],
                  ALU.mult, ALU.add)
            P.ts(UA[:, 512:1024], UA[:, 512:1024], 65535.0, None, ALU.min)
            t_m2 = P.tt(gsl, gsl, UA[:, 512:1024], ALU.mult, inc=ve)
            t_rd = P.red(SCRR, gsl, ALU.max, absval=True, wait=[t_m2], inc=ve)
            P.tt(MXP, MXP, SCRR, ALU.max, wait=[t_mx2], inc=ve)
            t_sp = P.dma("gpsimd", m2_sc[:, st_i * 512:(st_i + 1) * 512], gsl, st,
                         wait=[t_m2, t_rd])
            sp_free[sl] = t_sp
        t_p4 = finish_partial(0)
        t_m4 = global_max(1, [t_p4], None)
        t_mb4 = bcast_scalar(0, MBC, [t_m4])
        P.tt(S4B, MBC, K3B, ALU.mult, wait=[t_mb4])
        P.op("vector", lambda e: e.tensor_scalar(S4B, S4B, R327, 1e-12, ALU.mult, ALU.max))
        P.recip(RS4B, S4B)
        P.tt(C34B, K3B, RS4B, ALU.mult)
        t_c34 = P.tt(K45B, S4B, S2B, ALU.mult, inc=ve)

        P.phase("l1f3")
        P.memset(MXP, 0.0, inc=gp)
        t_mx3 = (gp, gp.n)
        sp_free = [None, None]
        for st_i in range(NST):
            sl = st_i % 2
            msl = XNKQ[:, sl * 1024:sl * 1024 + 512]
            usl = XNKQ[:, 2048 + sl * 1024:2048 + sl * 1024 + 512]
            t_m = P.dma("sync", msl, m2_sc[:, st_i * 512:(st_i + 1) * 512], ld,
                        wait=[sp_free[sl]] if sp_free[sl] else [(ve, ve.n), (ac, ac.n)])
            t_u = P.dma("sync", usl, upre_sc[:, st_i * 512:(st_i + 1) * 512], ld,
                        wait=[sp_free[sl]] if sp_free[sl] else [])
            t_w = P.act(UA[:, 0:512], msl, AF.Identity, scale=C34B,
                        wait=[t_m, t_c34, (ve, ve.n)], inc=ac)
            P.ts(SQS[:, 0:512], UA[:, 0:512], C32, C32, ALU.add, ALU.subtract, wait=[t_w])
            P.ts(UA[:, 512:1024], UA[:, 0:512], C64, C64, ALU.add, ALU.subtract)
            P.stt(msl, SQS[:, 0:512], GAM, UA[:, 512:1024], ALU.mult, ALU.add, inc=ve)
            t_w2 = P.act(UA[:, 0:512], usl, AF.Identity, scale=RS2B,
                         wait=[t_u, (ve, ve.n)], inc=ac)
            P.ts(SQS[:, 0:512], UA[:, 0:512], C32, C32, ALU.add, ALU.subtract, wait=[t_w2])
            P.ts(UA[:, 512:1024], UA[:, 0:512], C64, C64, ALU.add, ALU.subtract)
            P.stt(usl, SQS[:, 0:512], GAM, UA[:, 512:1024], ALU.mult, ALU.add)
            t_mo = P.tt(msl, msl, usl, ALU.mult, inc=ve)
            t_rd = P.red(SCRR, msl, ALU.max, absval=True, wait=[t_mo], inc=ve)
            P.tt(MXP, MXP, SCRR, ALU.max, wait=[t_mx3], inc=ve)
            t_sp = P.dma("gpsimd", gpre_sc[:, st_i * 512:(st_i + 1) * 512], msl, st,
                         wait=[t_mo, t_rd])
            sp_free[sl] = t_sp
        t_p5 = finish_partial(0)
        t_m5 = global_max(1, [t_p5], None)
        t_mb5 = bcast_scalar(0, MBC, [t_m5])
        P.tt(S5B, MBC, K45B, ALU.mult, wait=[t_mb5])
        P.op("vector", lambda e: e.tensor_scalar(S5B, S5B, R327, 1e-12, ALU.mult, ALU.max))
        P.recip(RS4B, S5B)
        t_c45 = P.tt(C45B, K45B, RS4B, ALU.mult, inc=ve)

        P.phase("l1f4")
        wd_prev = [t_wd, t_wd]
        q5_free = [None, None]
        for ch in range(NCH1):
            cols = slice(ch * CT1, (ch + 1) * CT1)
            t_last = None
            for fb in range(NFB1):
                sl = fb % 2
                msl = XNKQ[:, sl * 512:sl * 512 + CT1]
                t_m = P.dma("sync", msl, gpre_sc[:, fb * T + cols.start:fb * T + cols.stop],
                            ld, wait=[q5_free[sl]] if q5_free[sl] else
                            [(ve, ve.n), (ac, ac.n)])
                t_w = P.act(UA[:, 0:CT1], msl, AF.Identity, scale=C45B,
                            wait=[t_m, t_c45, (ve, ve.n)], inc=ac)
                P.ts(SQS[:, 0:CT1], UA[:, 0:CT1], C32, C32, ALU.add, ALU.subtract,
                     wait=[t_w])
                P.ts(UA[:, 512:512 + CT1], UA[:, 0:CT1], C64, C64, ALU.add, ALU.subtract)
                t_q5 = P.stt(msl, SQS[:, 0:CT1], GAM, UA[:, 512:512 + CT1],
                             ALU.mult, ALU.add, inc=ve)
                t_wd2 = P.dma("sync", WS[:, 2048 + sl * 1024:2048 + (sl + 1) * 1024],
                              wdt_sc[:, fb * 1024:(fb + 1) * 1024], ld,
                              wait=[wd_prev[sl]] if wd_prev[sl] else [])
                for oc in range(8):
                    ob = BK[oc]
                    w0 = [t_q5, t_wd2]
                    if fb == 0:
                        w0 = w0 + ob.wwait()
                    t_last = P.mm(ob.pb[0:128, 0:CT1],
                                  WS[:, 2048 + sl * 1024 + oc * 128:2048 + sl * 1024 + (oc + 1) * 128],
                                  msl, start=(fb == 0), stop=(fb == NFB1 - 1),
                                  wait=w0, inc=te if oc == 7 else None)
                    if fb == NFB1 - 1:
                        ob.wrote(t_last)
                q5_free[sl] = t_last
                wd_prev[sl] = t_last
            for oc in range(8):
                ob = BK[oc]
                t_ra = P.stt(RES[:, oc * T + cols.start:oc * T + cols.stop],
                             ob.pb[0:128, 0:CT1], S5B,
                             RES[:, oc * T + cols.start:oc * T + cols.stop],
                             ALU.mult, ALU.add, wait=[t_last], inc=ve)
                ob.read(t_ra)
        resid_ready = (ve, ve.n)

    P.phase("out")
    res_done = [(ve, ve.n), (te, te.n), (ac, ac.n)]
    t_out = None
    for c in range(8):
        t_out = P.dma("gpsimd", y_h[c * 128:(c + 1) * 128, :], RES[:, c * T:(c + 1) * T],
                      st, wait=res_done)
    dump_req = [d[5:].split(",") for d in dbg if d.startswith("dump:")]
    if dump_req:
        avail = {"XNKQ": XNKQ[:, :], "UA": UA[:, :], "KV0": KV0[:, :],
                 "STA": STA[:, :], "EQF": EQF[:, :], "WS": WS[:, :],
                 "SQS": SQS[:, :], "CONST": CONST[:, :], "STB": STB[:, :],
                 "MKA": MKA, "DKT": DKT, "NDKT": NDKT, "MKROW": MKROW,
                 "MH16": MH16, "EHB": EHB, "EWTM": EWTM, "BTT": BTT,
                 "RDEN": RDEN, "DQF": DQF, "LNR": LNR[:, :]}
        for nm in dump_req[0]:
            src = avail[nm]
            dt_ = nc.dram_tensor("dump_" + nm, list(src.shape), F32,
                                 kind="ExternalOutput").ap()
            t_out = P.dma("gpsimd", dt_[:, :], src, st, wait=res_done)
    P.op("gpsimd", lambda e: e.engine_nop(), wait=[t_out])

    P.emit()
    return nc


_CACHE = {}


def _consts():
    H = 16
    ident = np.eye(128, dtype=np.float32)
    hones = np.zeros((128, 8 * H), dtype=np.float32)
    for c in range(8):
        for p in range(128):
            hones[p, c * 16 + 2 * c + (p >= 64)] = 0.5
    indb = np.zeros((16, 8 * 128), dtype=np.float32)
    for c in range(8):
        for p in range(128):
            indb[2 * c + (p >= 64), c * 128 + p] = 1.0
    indh = np.zeros((16, 16 * 64), dtype=np.float32)
    for h in range(16):
        indh[h, h * 64:(h + 1) * 64] = 1.0
    return ident, hones, indb, indh


def make_in_maps(inputs, n_cores=8):
    x = np.asarray(inputs["L0_attn_x"], dtype=np.float32)
    B, N, D = x.shape
    T = (B * N) // n_cores
    ident, hones, indb, indh = _consts()
    common = {
        "l0_wqkv": np.ascontiguousarray(inputs["L0_attn_W_qkv"], dtype=np.float32),
        "l0_wout": np.ascontiguousarray(inputs["L0_attn_W_out"], dtype=np.float32),
        "l0_proj": np.ascontiguousarray(inputs["L0_attn_proj"], dtype=np.float32),
        "l0_w1": np.ascontiguousarray(inputs["L0_ffn_W1"], dtype=np.float32),
        "l0_b1": np.ascontiguousarray(inputs["L0_ffn_b1"], dtype=np.float32),
        "l0_w2": np.ascontiguousarray(inputs["L0_ffn_W2"], dtype=np.float32),
        "l0_b2": np.ascontiguousarray(inputs["L0_ffn_b2"], dtype=np.float32),
        "l1_qkvw": np.ascontiguousarray(inputs["L1_attn_qkv_w"], dtype=np.float32),
        "l1_qkvb": np.ascontiguousarray(inputs["L1_attn_qkv_b"], dtype=np.float32),
        "l1_projw": np.ascontiguousarray(inputs["L1_attn_proj_w"], dtype=np.float32),
        "l1_projb": np.ascontiguousarray(inputs["L1_attn_proj_b"], dtype=np.float32),
        "l1_pm": np.ascontiguousarray(inputs["L1_attn_projection_matrix"], dtype=np.float32),
        "l1_wg": np.ascontiguousarray(inputs["L1_ffn_w_gate"], dtype=np.float32),
        "l1_wu": np.ascontiguousarray(inputs["L1_ffn_w_up"], dtype=np.float32),
        "l1_wd": np.ascontiguousarray(inputs["L1_ffn_w_down"], dtype=np.float32),
        "ident": ident, "hones": hones, "indb": indb, "indh": indh,
    }
    halves = n_cores // B
    in_maps = []
    for c in range(n_cores):
        b, hf = c // halves, c % halves
        xc = np.ascontiguousarray(x[b, hf * T:(hf + 1) * T, :].T)
        m = dict(common)
        m["x"] = xc
        in_maps.append(m)
    return in_maps, T, B, N, D


def kernel(**inputs):
    in_maps, T, B, N, D = make_in_maps(inputs)
    key = (T,)
    if key not in _CACHE:
        _CACHE[key] = build_nc(T=T)
    nc = _CACHE[key]
    res = run_bass_kernel_spmd(nc, in_maps, core_ids=list(range(8)))
    out = np.empty((B, N, D), dtype=np.float32)
    for c in range(8):
        b, hf = c // 2, c % 2
        out[b, hf * T:(hf + 1) * T, :] = res.results[c]["y"].T
    return out
